# version 01
# baseline (speedup 1.0000x reference)
import math
import numpy as np
import ml_dtypes
import concourse.bass as bass
import concourse.mybir as mybir
from concourse.alu_op_type import AluOpType
from concourse.bass_utils import run_bass_kernel_spmd

F32 = mybir.dt.float32
BF16 = mybir.dt.bfloat16
I32 = mybir.dt.int32
AF = mybir.ActivationFunctionType
OP = AluOpType
AX = mybir.AxisListType.X
BF = ml_dtypes.bfloat16

B = 2
S = 8192
D = 1024
TOK = 2048
NCORES = 8
HEADS = 16
HD = 64
AH = 4
FFN = 4096
NH = 32
SH = 8
P_ = 64
NST = 64
CH = 64
DMLP = 2560
DFF = 4096
NB = 32
EVLEN = 1152
NEAR_D0 = [-384, -256, -128, 0, 128]

_REL_PERM = np.array([i * 2 for i in range(32)] + [i * 2 + 1 for i in range(32)])


def _bucket_of(d):
    if d < 16:
        return d
    v = np.float32(np.log(np.float32(d) / np.float32(16.0)))
    v = v / np.float32(math.log(128 / 16)) * np.float32(16)
    b = 16 + int(v)
    return min(b, 31)


def _host_constants():
    C = {}
    sel = np.zeros((NB, EVLEN), np.float32)
    for idx in range(EVLEN):
        d = idx - 511
        if d >= 1:
            sel[_bucket_of(d), idx] = 1.0
    C["c_sel"] = sel.astype(BF)
    mask = np.zeros((1, EVLEN), np.float32)
    mask[0, : 511 + 1] = -30000.0
    C["c_maskrow"] = mask.astype(BF)
    C["c_J"] = np.eye(128, dtype=BF)[::-1].copy()
    C["c_I"] = np.eye(128, dtype=BF)
    C["c_If"] = np.eye(128, dtype=np.float32)
    C["c_onescol_f"] = np.ones((128, 1), np.float32)
    C["c_onescol_b"] = np.ones((128, 1), BF)
    C["c_onesrow_f"] = np.ones((1, 128), np.float32)
    C["c_onesrow_b"] = np.ones((1, 128), BF)
    s2 = np.zeros((128, 2), BF)
    s2[0:64, 0] = 1
    s2[64:128, 1] = 1
    C["c_sel64"] = s2
    e2 = np.zeros((2, 128), BF)
    e2[0, 0:64] = 1
    e2[1, 64:128] = 1
    C["c_exp2"] = e2
    e8 = np.zeros((4, 8, 128), BF)
    for j in range(4):
        e8[j, 2 * j, 0:64] = 1
        e8[j, 2 * j + 1, 64:128] = 1
    C["c_exppair"] = e8.reshape(32, 128)
    eA = np.zeros((8, 128), BF)
    eB = np.zeros((8, 128), BF)
    for p in range(128):
        eA[p // 32, p] = 1
        eB[4 + p // 32, p] = 1
    C["c_exp8A"] = eA
    C["c_exp8B"] = eB
    C["c_tril"] = np.triu(np.ones((128, 128), np.float32))
    C["c_ramp"] = (64.0 - (np.arange(512) % 64)).astype(np.float32)[None, :]
    m8 = np.zeros((8, 256), np.float32)
    for h in range(8):
        m8[h, (h // 2) * 64:(h // 2) * 64 + 64] = 1
    C["c_mask8"] = m8
    ax = np.zeros((8, 128), BF)
    for h in range(8):
        ax[h, (h % 2) * 64:(h % 2) * 64 + 64] = 1
    C["c_aexp"] = ax
    return C


def _stage_inputs(inp):
    consts = _host_constants()
    f32 = np.float32
    AS = np.ascontiguousarray

    def vec8(v):
        return AS(np.asarray(v, f32).reshape(8, 128).T)

    x = np.asarray(inp["L0_attn_x"], f32)
    wq = np.asarray(inp["L0_attn_Wq"], f32)
    wk = np.asarray(inp["L0_attn_Wk"], f32)
    wv = np.asarray(inp["L0_attn_Wv"], f32)
    relb = np.asarray(inp["L0_attn_rel_bias"], f32)
    scale = np.asarray(inp["L0_attn_scale"], f32)
    win = np.asarray(inp["L1_attn_W_in"], f32)
    bb = np.asarray(inp["L1_attn_B_bias"], f32)[0]
    cb = np.asarray(inp["L1_attn_C_bias"], f32)[0]
    wupT = AS(np.asarray(inp["L1_ffn_w_up"], f32).T)
    wdnT = AS(np.asarray(inp["L1_ffn_w_down"], f32).T)
    cbt = AS(np.asarray(inp["L0_ffn_CB"], f32).T)
    z0, x0, b0, c0, d0, t0 = 0, 2048, 4096, 6144, 8192, 8224

    maps = []
    for c in range(NCORES):
        b = c // 4
        q = c % 4
        h4 = list(range(4 * q, 4 * q + 4))
        h8 = list(range(8 * q, 8 * q + 8))
        m = dict(consts)
        m["xq"] = AS(x[b, TOK * q: TOK * (q + 1), :].T)
        m["ln1w8"] = vec8(inp["L0_attn_ln1_w"])
        m["ln2w8"] = vec8(inp["L0_attn_ln2_w"])
        m["n1w8"] = vec8(inp["L1_attn_norm1_w"])
        m["n2w8"] = vec8(inp["L1_attn_norm2_w"])
        m["scb8"] = vec8(inp["L0_ffn_SCB"])
        m["bias8"] = vec8(inp["L0_ffn_bias"])
        cols4 = np.concatenate([np.arange(h * HD, (h + 1) * HD) for h in h4])
        m["wq4"] = AS(wq[:, cols4])
        m["wk4"] = AS(wk[:, cols4])
        m["wv4"] = AS(wv[:, cols4])
        m["scale4"] = AS(scale[h4][:, None])
        m["scale22"] = AS(scale[h4].reshape(2, 2).T)
        m["scale4r"] = AS(scale[h4][None, :])
        m["relb4"] = AS(relb[:, h4])
        m["wo"] = np.asarray(inp["L0_attn_Wo"], f32)
        m["w1"] = np.asarray(inp["L0_attn_W1"], f32)
        m["w2"] = np.asarray(inp["L0_attn_W2"], f32)
        m["cbt"] = cbt
        zc = np.concatenate([np.arange(z0 + h * P_, z0 + (h + 1) * P_) for h in h8])
        xc = np.concatenate([np.arange(x0 + h * P_, x0 + (h + 1) * P_) for h in h8])
        bcol = np.concatenate([b0 + h * NST + _REL_PERM for h in h8])
        ccol = np.concatenate([c0 + h * NST + _REL_PERM for h in h8])
        dtc = d0 + np.array(h8)
        thc = np.concatenate([np.arange(t0 + h * 32, t0 + (h + 1) * 32) for h in h8])
        m["win_z"] = AS(win[:, zc])
        m["win_x"] = AS(win[:, xc])
        m["win_B"] = AS(win[:, bcol])
        m["win_C"] = AS(win[:, ccol])
        m["win_dt"] = AS(win[:, dtc])
        m["win_th"] = AS(win[:, thc])
        m["alog8"] = AS(np.asarray(inp["L1_attn_A_log"], f32)[h8][:, None])
        m["dtb8"] = AS(np.asarray(inp["L1_attn_dt_bias"], f32)[h8][:, None])
        m["d8"] = AS(np.asarray(inp["L1_attn_D_param"], f32)[h8][None, :])
        m["bnw"] = AS(np.asarray(inp["L1_attn_Bn_w"], f32)[_REL_PERM][None, :])
        m["cnw"] = AS(np.asarray(inp["L1_attn_Cn_w"], f32)[_REL_PERM][None, :])
        m["bbias"] = AS(bb[h8][:, _REL_PERM].reshape(1, 512))
        m["cbias"] = AS(cb[h8][:, _REL_PERM].reshape(1, 512))
        m["wout"] = np.asarray(inp["L1_attn_W_out"], f32)
        m["wg"] = np.asarray(inp["L1_attn_Wg"], f32)
        m["wu"] = np.asarray(inp["L1_attn_Wu"], f32)
        m["wd"] = np.asarray(inp["L1_attn_Wd"], f32)
        m["wupT_sl"] = AS(wupT[:, 1024 * q: 1024 * (q + 1)])
        m["wdnT_sl"] = AS(wdnT[:, 256 * q: 256 * (q + 1)])
        maps.append(m)
    return maps


class Ctx:

    def __init__(self, nc):
        self.nc = nc
        self.cnt = {}
        self.sems = {}
        self._guards = []
        self.consts = {}

    def sem(self, name):
        if name not in self.sems:
            g = self.nc.semaphore(name)
            self.sems[name] = g.__enter__()
            self._guards.append(g)
            self.cnt[name] = 0
        return self.sems[name]

    def dma(self, queue, out, in_, semname):
        s = self.sem(semname)
        eng = self.nc.sync if queue == "sq" else self.nc.gpsimd
        eng.dma_start(out=out, in_=in_).then_inc(s, 16)
        self.cnt[semname] += 16

    def tag(self, instr, semname, v=1):
        s = self.sem(semname)
        instr.then_inc(s, v)
        self.cnt[semname] += v

    def wait(self, eng, semname):
        eng.wait_ge(self.sem(semname), self.cnt[semname])

    def bar(self):
        self.nc.all_engine_barrier()

    def prep_consts(self, vals):
        self.ccol = getattr(self, "ccol", {})
        for v in vals:
            if v not in self.ccol:
                tname = "cc%d" % len(self.ccol)
                tile = self.nc.alloc_sbuf_tensor(tname, [128, 1], F32)
                self.nc.vector.memset(tile[:], float(v))
                self.ccol[v] = tile

    def cc(self, v):
        return self.ccol[v][:, 0:1]

    def load_consts(self, names):
        nc = self.nc
        shp = {"c_sel": ([NB, EVLEN], BF16), "c_maskrow": ([1, EVLEN], BF16),
               "c_J": ([128, 128], BF16), "c_I": ([128, 128], BF16),
               "c_If": ([128, 128], F32),
               "c_onescol_f": ([128, 1], F32), "c_onescol_b": ([128, 1], BF16),
               "c_onesrow_f": ([1, 128], F32), "c_onesrow_b": ([1, 128], BF16),
               "c_sel64": ([128, 2], BF16), "c_exp2": ([2, 128], BF16),
               "c_exppair": ([32, 128], BF16), "c_exp8A": ([8, 128], BF16),
               "c_exp8B": ([8, 128], BF16), "c_tril": ([128, 128], F32),
               "c_ramp": ([1, 512], F32), "c_mask8": ([8, 256], F32),
               "c_aexp": ([8, 128], BF16)}
        self.cdram = getattr(self, "cdram", {})
        for n in names:
            sh, dt = shp[n]
            ap = self.nc.dram_tensor(n, sh, dt, kind="ExternalInput").ap()
            self.cdram[n] = ap
            sb = nc.alloc_sbuf_tensor("s" + n, sh, dt)
            self.dma("sq", sb[:], ap, "ldc")
            self.consts[n] = sb
        self.wait(nc.sync, "ldc")

    def c(self, name):
        return self.consts[name]


def _rows_chain_ln(ctx, psSx, psSxx, rO, extra=None):
    nc = ctx.nc
    sl = rO["sl"]
    nc.vector.tensor_scalar_mul(rO["mean"][0:1, sl], psSx[0:1, 0:512], 1.0 / D)
    nc.vector.tensor_scalar_mul(rO["var"][0:1, sl], psSxx[0:1, 0:512], 1.0 / D)
    ctx.bar()
    nc.vector.tensor_mul(rO["tmp"][0:1, sl], rO["mean"][0:1, sl], rO["mean"][0:1, sl])
    ctx.bar()
    nc.vector.tensor_sub(rO["var"][0:1, sl], rO["var"][0:1, sl], rO["tmp"][0:1, sl])
    ctx.bar()
    nc.scalar.activation(rO["tmp"][0:1, sl], rO["var"][0:1, sl], AF.Sqrt,
                         bias=ctx.cc(1e-5)[0:1, :], scale=1.0)
    ctx.bar()
    nc.vector.reciprocal(rO["r1"][0:1, sl], rO["tmp"][0:1, sl])
    ctx.bar()


def _emit_ln_sums(ctx, xsb, fchunks, tok, scr, psSx, psSxx, tslice, src_off=0):
    nc = ctx.nc
    ones = ctx.c("c_onescol_f")
    for fc in range(fchunks):
        xs = xsb[:, src_off + fc * tok + tslice.start: src_off + fc * tok + tslice.stop]
        a = nc.scalar.activation(scr[:, 0:512], xs, AF.Square, bias=0.0, scale=1.0)
        ctx.tag(a, "sq_ev")
        nc.tensor.wait_ge(ctx.sem("sq_ev"), ctx.cnt["sq_ev"])
        m1 = nc.tensor.matmul(psSx[0:1, 0:512], ones[:, 0:1], xs,
                              start=(fc == 0), stop=(fc == fchunks - 1))
        m2 = nc.tensor.matmul(psSxx[0:1, 0:512], ones[:, 0:1], scr[:, 0:512],
                              start=(fc == 0), stop=(fc == fchunks - 1))
        ctx.tag(m2, "mm_ev")
        nc.scalar.wait_ge(ctx.sem("mm_ev"), ctx.cnt["mm_ev"])


def build_p1():
    nc = bass.Bass("TRN2", target_bir_lowering=False, debug=False, detect_race_conditions=False)
    ctx = Ctx(nc)
    t = lambda n, s, d, k: nc.dram_tensor(n, s, d, kind=k).ap()
    xq = t("xq", [D, TOK], F32, "ExternalInput")
    ln1w8 = t("ln1w8", [128, 8], F32, "ExternalInput")
    wup_sl = t("wupT_sl", [D, 1024], F32, "ExternalInput")
    wdn_sl = t("wdnT_sl", [DFF, 256], F32, "ExternalInput")
    o_xn1 = t("o_xn1", [D, TOK], BF16, "ExternalOutput")
    o_rows = t("o_rows", [4, TOK], F32, "ExternalOutput")
    o_absp = t("o_absp", [1, 2], F32, "ExternalOutput")

    xsb = nc.alloc_sbuf_tensor("xsb", [128, 8 * TOK], F32)
    scr = nc.alloc_sbuf_tensor("scr", [128, 512], F32)
    w8 = nc.alloc_sbuf_tensor("w8", [128, 8], F32)
    xn1 = nc.alloc_sbuf_tensor("xn1", [128, 8 * TOK], BF16)
    r_mean = nc.alloc_sbuf_tensor("r_mean", [1, TOK], F32)
    r_var = nc.alloc_sbuf_tensor("r_var", [1, TOK], F32)
    r_r1 = nc.alloc_sbuf_tensor("r_r1", [1, TOK], F32)
    r_tmp = nc.alloc_sbuf_tensor("r_tmp", [1, TOK], F32)
    r_s = nc.alloc_sbuf_tensor("r_s", [1, TOK], F32)
    r_ms = nc.alloc_sbuf_tensor("r_ms", [1, TOK], F32)
    r_mr1 = nc.alloc_sbuf_tensor("r_mr1", [1, TOK], F32)
    psSx = nc.alloc_psum_tensor("psSx", [1, 512], F32)
    psSxx = nc.alloc_psum_tensor("psSxx", [1, 512], F32)
    psB = nc.alloc_psum_tensor("psB", [128, 512], F32)

    ctx.load_consts(["c_onescol_f", "c_onesrow_f"])
    ctx.prep_consts([1e-5, 1e-6])
    ctx.dma("sq", xsb[:].rearrange("p (f t) -> p f t", f=8),
            xq.rearrange("(f p) t -> p f t", p=128), "ld")
    ctx.dma("sq", w8[:], ln1w8, "ld")
    ctx.wait(nc.sync, "ld")
    ctx.bar()

    rO = {"mean": r_mean, "var": r_var, "r1": r_r1, "tmp": r_tmp}
    for tt in range(TOK // 512):
        tsl = slice(tt * 512, (tt + 1) * 512)
        _emit_ln_sums(ctx, xsb, 8, TOK, scr, psSx, psSxx, tsl)
        ctx.bar()
        rO["sl"] = tsl
        _rows_chain_ln(ctx, psSx, psSxx, rO)
    ctx.bar()
    nc.vector.tensor_mul(r_tmp[0:1, :], r_r1[0:1, :], r_r1[0:1, :])
    ctx.bar()
    nc.vector.tensor_mul(r_tmp[0:1, :], r_tmp[0:1, :], r_var[0:1, :])
    ctx.bar()
    nc.scalar.activation(r_tmp[0:1, :], r_tmp[0:1, :], AF.Sqrt,
                         bias=ctx.cc(1e-6)[0:1, :], scale=1.0)
    ctx.bar()
    nc.vector.reciprocal(r_s[0:1, :], r_tmp[0:1, :])
    ctx.bar()
    nc.vector.tensor_mul(r_s[0:1, :], r_s[0:1, :], r_r1[0:1, :])
    ctx.bar()
    nc.vector.tensor_mul(r_ms[0:1, :], r_s[0:1, :], r_mean[0:1, :])
    nc.vector.tensor_mul(r_mr1[0:1, :], r_r1[0:1, :], r_mean[0:1, :])
    ctx.bar()
    onesrow = ctx.c("c_onesrow_f")
    for tt in range(TOK // 512):
        tsl = slice(tt * 512, (tt + 1) * 512)
        m = nc.tensor.matmul(psB[:, 0:512], onesrow[0:1, :], r_s[0:1, tsl],
                             start=True, stop=True)
        ctx.tag(m, "bc_ev")
        nc.vector.wait_ge(ctx.sem("bc_ev"), ctx.cnt["bc_ev"])
        last = None
        for fc in range(8):
            off = fc * TOK + tt * 512
            nc.vector.tensor_tensor(scr[:, 0:512], xsb[:, off:off + 512],
                                    psB[:, 0:512], OP.mult)
            last = nc.vector.tensor_scalar(xn1[:, off:off + 512], scr[:, 0:512],
                                           w8[:, fc:fc + 1], None, OP.mult)
        ctx.tag(last, "xn_ev")
        nc.tensor.wait_ge(ctx.sem("xn_ev"), ctx.cnt["xn_ev"])
    ctx.bar()
    ctx.dma("sq", o_xn1.rearrange("(f p) t -> p f t", p=128),
            xn1[:].rearrange("p (f t) -> p f t", f=8), "st")
    ctx.dma("sq", o_rows[0:1, :], r_s[0:1, :], "st")
    ctx.dma("sq", o_rows[1:2, :], r_ms[0:1, :], "st")
    ctx.dma("sq", o_rows[2:3, :], r_r1[0:1, :], "st")
    ctx.dma("sq", o_rows[3:4, :], r_mr1[0:1, :], "st")

    wld = nc.alloc_sbuf_tensor("wld", [128, 1024], F32)
    absw = nc.alloc_sbuf_tensor("absw", [128, 1024], F32)
    psS2 = nc.alloc_psum_tensor("psS2", [1, 1024], F32)
    accr = nc.alloc_sbuf_tensor("accr", [1, 4], F32)
    ones = ctx.c("c_onescol_f")
    for mi, (ap, kdim, cols, po) in enumerate([(wup_sl, D, 1024, 0), (wdn_sl, DFF, 256, 512)]):
        ntile = kdim // 128
        for k in range(ntile):
            ctx.dma("sq", wld[:, 0:cols], ap[k * 128:(k + 1) * 128, :], "wl")
            ctx.wait(nc.scalar, "wl")
            a = nc.scalar.activation(absw[:, 0:cols], wld[:, 0:cols], AF.Abs,
                                     bias=0.0, scale=1.0)
            ctx.tag(a, "ab_ev")
            nc.tensor.wait_ge(ctx.sem("ab_ev"), ctx.cnt["ab_ev"])
            last_mm = None
            for j in range(max(1, cols // 512)):
                w = min(512, cols)
                last_mm = nc.tensor.matmul(
                    psS2[0:1, po + 0:po + w], ones[:, 0:1],
                    absw[:, j * 512:j * 512 + w],
                    start=(k == 0 and j == 0), stop=(k == ntile - 1 and j == max(0, cols // 512 - 1)))
            ctx.tag(last_mm, "wm_ev")
            nc.sync.wait_ge(ctx.sem("ab_ev"), ctx.cnt["ab_ev"])
            nc.scalar.wait_ge(ctx.sem("wm_ev"), ctx.cnt["wm_ev"])
    ctx.bar()
    nc.vector.tensor_reduce(accr[0:1, 0:1], psS2[0:1, 0:512], AX, OP.add)
    nc.vector.tensor_reduce(accr[0:1, 1:2], psS2[0:1, 512:768], AX, OP.add)
    ctx.bar()
    ctx.dma("sq", o_absp, accr[0:1, 0:2], "st")
    ctx.wait(nc.sync, "st")
    ctx.bar()
    return nc


def _p2_finish(ctx, o_y, ysb):
    nc = ctx.nc
    ctx.bar()
    for h in range(4):
        ctx.dma("sq", o_y[h * 64:(h + 1) * 64, :],
                ysb[(h % 2) * 64:(h % 2) * 64 + 64, (h // 2) * S:(h // 2) * S + S], "st")
    ctx.wait(nc.sync, "st")
    ctx.bar()
    return nc


def build_p2():
    import os
    _UPTO = int(os.environ.get("P2_UPTO", "9"))
    nc = bass.Bass("TRN2", target_bir_lowering=False, debug=False, detect_race_conditions=False)
    ctx = Ctx(nc)
    t = lambda n, s_, d, k: nc.dram_tensor(n, s_, d, kind=k).ap()
    xn1f = t("xn1f", [4, D, TOK], BF16, "ExternalInput")
    msf = t("msf", [1, S], F32, "ExternalInput")
    wq4 = t("wq4", [D, 256], F32, "ExternalInput")
    wk4 = t("wk4", [D, 256], F32, "ExternalInput")
    wv4 = t("wv4", [D, 256], F32, "ExternalInput")
    ln1w8 = t("ln1w8", [128, 8], F32, "ExternalInput")
    scale4 = t("scale4", [4, 1], F32, "ExternalInput")
    scale22 = t("scale22", [2, 2], F32, "ExternalInput")
    scale4r = t("scale4r", [1, 4], F32, "ExternalInput")
    relb4 = t("relb4", [NB, 4], F32, "ExternalInput")
    o_y = t("o_y", [256, S], BF16, "ExternalOutput")
    ev_hbm = nc.dram_tensor("ev_hbm", [4, EVLEN], BF16, kind="Internal")

    ctx.load_consts(["c_sel", "c_maskrow", "c_J", "c_I", "c_If", "c_onescol_f",
                     "c_onescol_b", "c_onesrow_f", "c_onesrow_b", "c_sel64",
                     "c_exp2"])
    ctx.prep_consts([1e-6])

    wqb = nc.alloc_sbuf_tensor("wqb", [128, 8 * 256], BF16)
    wkb = nc.alloc_sbuf_tensor("wkb", [128, 8 * 256], BF16)
    wvb = nc.alloc_sbuf_tensor("wvb", [128, 8 * 256], BF16)
    ln1b = nc.alloc_sbuf_tensor("ln1b", [128, 8], BF16)
    ln1f = nc.alloc_sbuf_tensor("ln1f", [128, 8], F32)
    pool = nc.alloc_sbuf_tensor("pool", [128, 18 * 1024], BF16)
    cwsb = nc.alloc_sbuf_tensor("cwsb", [128, 6], BF16)
    cwrows = nc.alloc_sbuf_tensor("cwrows", [1, 6 * 128], BF16)
    xt = [pool[:, 0:4096], pool[:, 4096:8192]]
    msneg = pool[0:1, 8192:16384]
    qsb = nc.alloc_sbuf_tensor("qsb", [128, 2 * S], BF16)
    ksb = nc.alloc_sbuf_tensor("ksb", [128, 2 * S], BF16)
    vtok = nc.alloc_sbuf_tensor("vtok", [128, 64 * 260], BF16)
    sqsb = nc.alloc_sbuf_tensor("sqsb", [128, 512], BF16)
    rsqb = nc.alloc_sbuf_tensor("rsqb", [2, 512], BF16)
    rsqf = nc.alloc_sbuf_tensor("rsqf", [2, 512], F32)
    bcq = nc.alloc_sbuf_tensor("bcq", [128, 512], BF16)
    vtmp = nc.alloc_sbuf_tensor("vtmp", [128, 512], BF16)
    scl4 = nc.alloc_sbuf_tensor("scl4", [4, 1], F32)
    scl22 = nc.alloc_sbuf_tensor("scl22", [2, 2], F32)
    scl4r = nc.alloc_sbuf_tensor("scl4r", [1, 4], F32)
    rb4 = nc.alloc_sbuf_tensor("rb4", [NB, 4], F32)
    srel = nc.alloc_sbuf_tensor("srel", [NB, 4], F32)
    srelb = nc.alloc_sbuf_tensor("srelb", [NB, 4], BF16)
    srelT = nc.alloc_sbuf_tensor("srelT", [4, NB], F32)
    mrow = nc.alloc_sbuf_tensor("mrow", [4, 4], F32)
    mnegbc = nc.alloc_sbuf_tensor("mnegbc", [128, 8], F32)
    rowt = nc.alloc_sbuf_tensor("rowt", [1, 4], F32)
    evsb = nc.alloc_sbuf_tensor("evsb", [4, EVLEN], BF16)
    l2tmp = nc.alloc_sbuf_tensor("l2tmp", [128, 512], BF16)
    esb = pool[:, 0:10240]
    phat = pool[:, 10240:12800]
    phat2 = pool[:, 12800:15360]
    vmean = nc.alloc_sbuf_tensor("vmean", [64, 4], F32)
    srow = nc.alloc_sbuf_tensor("srow", [128, 512], F32)
    ones64 = nc.alloc_sbuf_tensor("ones64", [128, 64], F32)
    bcd = nc.alloc_sbuf_tensor("bcd", [64, 512], F32)
    ysb = nc.alloc_sbuf_tensor("ysb", [128, 2 * S], BF16)

    pbig = nc.alloc_psum_tensor("pbig", [128, 2048], F32)
    pbig2 = nc.alloc_psum_tensor("pbig2", [128, 1536], F32)
    ptb = nc.alloc_psum_tensor("ptb", [128, 256], BF16)

    ctx.dma("gq", wqb[:].rearrange("p (kc m) -> p kc m", kc=8),
            wq4.rearrange("(kc p) m -> p kc m", p=128), "gl")
    ctx.dma("gq", wkb[:].rearrange("p (kc m) -> p kc m", kc=8),
            wk4.rearrange("(kc p) m -> p kc m", p=128), "gl")
    ctx.dma("gq", wvb[:].rearrange("p (kc m) -> p kc m", kc=8),
            wv4.rearrange("(kc p) m -> p kc m", p=128), "gl")
    ctx.dma("sq", ln1f[:], ln1w8, "ld")
    ctx.dma("gq", msneg, msf, "gl")
    ctx.dma("sq", scl4[:], scale4, "ld")
    ctx.dma("sq", scl22[:], scale22, "ld")
    ctx.dma("sq", scl4r[:], scale4r, "ld")
    ctx.dma("sq", rb4[:], relb4, "ld")
    ctx.wait(nc.sync, "ld")
    ctx.wait(nc.gpsimd, "gl")
    ctx.bar()
    nc.vector.memset(vtok[:], 1.0)
    nc.vector.memset(ones64[64:65, :], 1.0)
    nc.vector.tensor_copy(ln1b[:], ln1f[:])
    nc.vector.tensor_scalar_mul(msneg, msneg, -1.0)
    ctx.bar()

    for wi, wb in enumerate([wqb, wkb, wvb]):
        for mi in range(2):
            for kc in range(8):
                nc.tensor.matmul(pbig[:, wi * 2 + mi: wi * 2 + mi + 1],
                                 wb[:, kc * 256 + mi * 128: kc * 256 + (mi + 1) * 128],
                                 ln1b[:, kc:kc + 1], start=(kc == 0), stop=(kc == 7))
    ctx.bar()
    nc.vector.tensor_copy(cwsb[:, 0:6], pbig[:, 0:6])
    ctx.bar()
    for j in range(6):
        nc.tensor.transpose(ptb[0:1, 0:128], cwsb[:, j:j + 1], ctx.c("c_I")[:])
        ctx.bar()
        nc.vector.tensor_copy(cwrows[0:1, j * 128:(j + 1) * 128], ptb[0:1, 0:128])
        ctx.bar()

    for tt in range(S // 512 if _UPTO >= 2 else 0):
        xb = xt[tt % 2]
        rank = (tt * 512) // TOK
        toff = (tt * 512) % TOK
        ctx.dma("sq", xb.rearrange("p (kc t) -> p kc t", kc=8),
                xn1f[rank, :, toff:toff + 512].rearrange("(kc p) t -> p kc t", p=128),
                "xl")
        ctx.wait(nc.sync, "xl")
        ctx.bar()
        for wi, wb in enumerate([wqb, wkb, wvb]):
            for mi in range(2):
                pb = pbig[:, (mi % 2) * 512:(mi % 2) * 512 + 512]
                for kc in range(8):
                    nc.tensor.matmul(pb, wb[:, kc * 256 + mi * 128: kc * 256 + (mi + 1) * 128],
                                     xb[:, kc * 512:(kc + 1) * 512],
                                     start=(kc == 0), stop=False)
                ci6 = wi * 2 + mi
                nc.tensor.matmul(pb, cwrows[0:1, ci6 * 128:(ci6 + 1) * 128],
                                 msneg[0:1, tt * 512:(tt + 1) * 512],
                                 start=False, stop=True)
                ctx.bar()
                if wi < 2:
                    nc.scalar.activation(sqsb[:, :], pb, AF.Square, bias=0.0, scale=1.0)
                    ctx.bar()
                    nc.tensor.matmul(pbig2[0:2, 0:512], ctx.c("c_sel64")[:, :], sqsb[:, :],
                                     start=True, stop=True)
                    ctx.bar()
                    nc.scalar.activation(rsqf[0:2, :], pbig2[0:2, 0:512], AF.Sqrt,
                                         bias=ctx.cc(1e-6)[0:2, :], scale=1.0)
                    ctx.bar()
                    nc.vector.reciprocal(rsqf[0:2, :], rsqf[0:2, :])
                    ctx.bar()
                    if wi == 0:
                        nc.vector.tensor_scalar(rsqf[0:2, :], rsqf[0:2, :],
                                                scl22[:, mi:mi + 1], None, OP.mult)
                        ctx.bar()
                    nc.vector.tensor_copy(rsqb[0:2, :], rsqf[0:2, :])
                    ctx.bar()
                    nc.tensor.matmul(pbig2[:, 512:1024], ctx.c("c_exp2")[:, :], rsqb[0:2, :],
                                     start=True, stop=True)
                    ctx.bar()
                    nc.vector.tensor_copy(bcq[:, :], pbig2[:, 512:1024])
                    ctx.bar()
                    dst = (qsb if wi == 0 else ksb)
                    nc.vector.tensor_tensor(dst[:, mi * S + tt * 512: mi * S + (tt + 1) * 512],
                                            pb, bcq[:, :], OP.mult)
                    ctx.bar()
                else:
                    nc.vector.tensor_copy(vtmp[:, :], pb)
                    ctx.bar()
                    for q4 in range(4):
                        tc = tt * 4 + q4
                        nc.tensor.transpose(ptb[:, 0:128], vtmp[:, q4 * 128:(q4 + 1) * 128],
                                            ctx.c("c_I")[:])
                        ctx.bar()
                        base = tc * 260
                        nc.vector.tensor_copy(vtok[:, base + (2 * mi) * 65: base + (2 * mi) * 65 + 64],
                                              ptb[:, 0:64])
                        nc.vector.tensor_copy(vtok[:, base + (2 * mi + 1) * 65: base + (2 * mi + 1) * 65 + 64],
                                              ptb[:, 64:128])
                        ctx.bar()

    if _UPTO < 3:
        return _p2_finish(ctx, o_y, ysb)
    nc.tensor.matmul(pbig[0:NB, 0:4], ctx.c("c_onesrow_f")[0:1, 0:NB], scl4r[0:1, :],
                     start=True, stop=True)
    ctx.bar()
    nc.vector.tensor_tensor(srel[:, :], rb4[:, :], pbig[0:NB, 0:4], OP.mult)
    ctx.bar()
    nc.vector.tensor_copy(srelb[:, :], srel[:, :])
    ctx.bar()
    nc.tensor.transpose(pbig[0:4, 0:NB], srel[:, :], ctx.c("c_If")[0:NB, 0:NB])
    ctx.bar()
    nc.vector.tensor_copy(srelT[:, :], pbig[0:4, 0:NB])
    ctx.bar()
    nc.vector.tensor_reduce(mrow[0:4, 0:1], srelT[:, :], AX, OP.max)
    nc.scalar.activation(mrow[0:4, 1:2], scl4[0:4, 0:1], AF.Abs, bias=0.0, scale=1.0)
    ctx.bar()
    nc.vector.tensor_scalar_mul(mrow[0:4, 1:2], mrow[0:4, 1:2], 1.02)
    ctx.bar()
    nc.vector.tensor_add(mrow[0:4, 2:3], mrow[0:4, 0:1], mrow[0:4, 1:2])
    ctx.bar()
    nc.vector.tensor_sub(mrow[0:4, 3:4], mrow[0:4, 2:3], srelT[:, 31:32])
    ctx.bar()
    for ci, col in enumerate([2, 3]):
        nc.tensor.transpose(pbig[0:1, 0:4], mrow[0:4, col:col + 1], ctx.c("c_If")[0:4, 0:4])
        ctx.bar()
        nc.vector.tensor_scalar_mul(rowt[0:1, :], pbig[0:1, 0:4], -1.0)
        ctx.bar()
        nc.tensor.matmul(pbig[:, 0:4], ctx.c("c_onesrow_f")[0:1, :], rowt[0:1, :],
                         start=True, stop=True)
        ctx.bar()
        nc.vector.tensor_copy(mnegbc[:, ci * 4:(ci + 1) * 4], pbig[:, 0:4])
        ctx.bar()
    for j, (o, w) in enumerate([(0, 512), (512, 512), (1024, 128)]):
        nc.tensor.matmul(pbig[0:4, o:o + w], srelb[:, :], ctx.c("c_sel")[:, o:o + w],
                         start=True, stop=False)
        nc.tensor.matmul(pbig[0:4, o:o + w], ctx.c("c_onesrow_b")[0:1, 0:4],
                         ctx.c("c_maskrow")[0:1, o:o + w], start=False, stop=True)
    ctx.bar()
    nc.scalar.activation(evsb[:, :], pbig[0:4, 0:EVLEN], AF.Exp, bias=0.0, scale=1.0)
    ctx.bar()
    ctx.dma("sq", ev_hbm.ap(), evsb[:, :], "ev")
    ctx.wait(nc.sync, "ev")
    ctx.bar()
    for h in range(4):
        for ci, d0 in enumerate(NEAR_D0):
            src = bass.AP(ev_hbm.ap().tensor, h * EVLEN + d0 + 384, [[1, 128], [1, 512]])
            ctx.dma("sq", l2tmp[:], src, "el")
            ctx.wait(nc.sync, "el")
            ctx.bar()
            nc.tensor.matmul(pbig[:, 0:512], ctx.c("c_J")[:], l2tmp[:], start=True, stop=True)
            ctx.bar()
            nc.vector.tensor_copy(esb[:, (h * 5 + ci) * 512:(h * 5 + ci + 1) * 512],
                                  pbig[:, 0:512])
            ctx.bar()

    if _UPTO < 4:
        return _p2_finish(ctx, o_y, ysb)
    for tc in range(64):
        for h in range(4):
            nc.tensor.matmul(pbig[0:64, h:h + 1],
                             vtok[:, tc * 260 + h * 65: tc * 260 + h * 65 + 64],
                             ctx.c("c_onescol_b")[:, 0:1],
                             start=(tc == 0 and h == 0), stop=(tc == 63 and h == 3))
    ctx.bar()
    nc.vector.tensor_scalar_mul(vmean[:, :], pbig[0:64, 0:4], 1.0 / S)
    ctx.bar()

    if _UPTO < 5:
        return _p2_finish(ctx, o_y, ysb)
    _NQT = int(os.environ.get("P2_NQT", "16"))
    _NH = int(os.environ.get("P2_NH", "4"))
    for h in range(_NH if _UPTO >= 5 else 0):
        prow = (h % 2) * 64
        fofs = (h // 2) * S
        for qt in range(_NQT):
            qs = qt * 512
            far = list(range(0, max(0, 4 * qt - 1)))
            near = [kc for kc in range(max(0, 4 * qt - 1), 4 * qt + 4)]
            groups = [far[i:i + 5] for i in range(0, len(far), 5)]
            groups = [(g, False) for g in groups if g] + [(near, True)]
            ntot = len(far) + len(near)
            nav = 0
            first_av = True
            for g, is_near in groups:
                nc.tensor.wait_ge(ctx.sem("ex"), ctx.cnt["ex"])
                for bi, kc in enumerate(g):
                    _i = nc.tensor.matmul(pbig[:, bi * 512:(bi + 1) * 512] if bi < 4 else
                                     pbig2[:, (bi - 4) * 512:(bi - 3) * 512],
                                     ksb[prow:prow + 64, fofs + kc * 128: fofs + (kc + 1) * 128],
                                     qsb[prow:prow + 64, fofs + qs: fofs + qs + 512],
                                     start=True, stop=True)
                ctx.tag(_i, "qk")
                nc.scalar.wait_ge(ctx.sem("qk"), ctx.cnt["qk"])
                mcol = h if is_near else 4 + h
                for bi in range(len(g)):
                    src = (pbig[:, bi * 512:(bi + 1) * 512] if bi < 4 else
                           pbig2[:, (bi - 4) * 512:(bi - 3) * 512])
                    _i = nc.scalar.activation(phat[:, bi * 512:(bi + 1) * 512], src, AF.Exp,
                                         bias=mnegbc[:, mcol:mcol + 1], scale=1.0)
                ctx.tag(_i, "ex")
                use = phat
                if is_near:
                    nc.vector.wait_ge(ctx.sem("ex"), ctx.cnt["ex"])
                    for bi, kc in enumerate(g):
                        d0 = qs - kc * 128
                        ci = NEAR_D0.index(d0)
                        _i = nc.vector.tensor_tensor(
                            phat2[:, bi * 512:(bi + 1) * 512],
                            phat[:, bi * 512:(bi + 1) * 512],
                            esb[:, (h * 5 + ci) * 512:(h * 5 + ci + 1) * 512], OP.mult)
                    ctx.tag(_i, "em")
                    nc.tensor.wait_ge(ctx.sem("em"), ctx.cnt["em"])
                    use = phat2
                else:
                    nc.tensor.wait_ge(ctx.sem("ex"), ctx.cnt["ex"])
                _i = None
                for bi, kc in enumerate(g if not os.environ.get("P2_NOAV") else []):
                    nav += 1
                    _i = nc.tensor.matmul(pbig2[0:65, 1024:1536],
                                     vtok[:, kc * 260 + h * 65: kc * 260 + (h + 1) * 65],
                                     use[:, bi * 512:(bi + 1) * 512],
                                     start=first_av, stop=(nav == ntot))
                    first_av = False
                if _i is not None:
                    ctx.tag(_i, "av")
                nc.scalar.wait_ge(ctx.sem("av"), ctx.cnt["av"])
            ctx.bar()
            if qt == 0 and not os.environ.get("P2_NOFIX"):
                nc.vector.tensor_copy(pbig2[0:64, 1024:1025], vmean[:, h:h + 1])
                nc.vector.memset(pbig2[64:65, 1024:1025], 1.0)
                ctx.bar()
            if os.environ.get("P2_NONORM"):
                continue
            nc.vector.reciprocal(srow[64:65, 0:512], pbig2[64:65, 1024:1536])
            ctx.bar()
            nc.tensor.matmul(pbig2[0:64, 512:1024], ones64[64:65, 0:64],
                             srow[64:65, 0:512], start=True, stop=True)
            ctx.bar()
            nc.vector.tensor_copy(bcd[:, :], pbig2[0:64, 512:1024])
            ctx.bar()
            nc.vector.tensor_tensor(ysb[prow:prow + 64, fofs + qs: fofs + qs + 512],
                                    pbig2[0:64, 1024:1536], bcd[:, :], OP.mult)
            ctx.bar()
    for h in range(4):
        ctx.dma("sq", o_y[h * 64:(h + 1) * 64, :],
                ysb[(h % 2) * 64:(h % 2) * 64 + 64, (h // 2) * S:(h // 2) * S + S], "st")
    ctx.wait(nc.sync, "st")
    ctx.bar()
    return nc


def build_p3():
    import os
    _DBG = os.environ.get("P3_DBG", "")
    nc = bass.Bass("TRN2", target_bir_lowering=False, debug=False, detect_race_conditions=False)
    ctx = Ctx(nc)
    t = lambda n, s_, d, k: nc.dram_tensor(n, s_, d, kind=k).ap()
    yq = t("yq", [D, TOK], BF16, "ExternalInput")
    xq = t("xq", [D, TOK], F32, "ExternalInput")
    rows1 = t("rows1", [4, TOK], F32, "ExternalInput")
    wo = t("wo", [D, D], F32, "ExternalInput")
    w1 = t("w1", [D, FFN], F32, "ExternalInput")
    w2 = t("w2", [FFN, D], F32, "ExternalInput")
    cbt = t("cbt", [D, D], F32, "ExternalInput")
    ln2w8 = t("ln2w8", [128, 8], F32, "ExternalInput")
    scb8 = t("scb8", [128, 8], F32, "ExternalInput")
    bias8 = t("bias8", [128, 8], F32, "ExternalInput")
    o_x4 = t("o_x4", [D, TOK], F32, "ExternalOutput")
    o_dbg = t("o_dbg", [D, TOK], F32, "ExternalOutput")

    def dump(stage, buf):
        if _DBG == stage:
            ctx.bar()
            ctx.dma("sq", o_dbg.rearrange("(f p) t -> p f t", p=128),
                    buf[:, 0:16384].rearrange("p (f t) -> p f t", f=8), "dbg")
            ctx.wait(nc.sync, "dbg")
            ctx.bar()

    xsb = nc.alloc_sbuf_tensor("xsb", [128, 8 * TOK], F32)
    yinner = nc.alloc_sbuf_tensor("yinner", [128, 8 * TOK], F32)
    hbuf = yinner[:].bitcast(BF16)[:, 0:16384]
    pool = nc.alloc_sbuf_tensor("pool3", [128, 16384], BF16)
    wslab = nc.alloc_sbuf_tensor("wslab", [128, 4096], BF16)
    scr = nc.alloc_sbuf_tensor("scr3", [128, 512], F32)
    cwrow = nc.alloc_sbuf_tensor("cwrow", [1, 8 * 128], BF16)
    cwcol = nc.alloc_sbuf_tensor("cwcol", [128, 8], BF16)
    w8a = nc.alloc_sbuf_tensor("w8a", [128, 8], F32)
    w8b = nc.alloc_sbuf_tensor("w8b", [128, 8], F32)
    w8c = nc.alloc_sbuf_tensor("w8c", [128, 8], F32)
    r_r1 = nc.alloc_sbuf_tensor("r_r1", [1, TOK], F32)
    r_mr1n = nc.alloc_sbuf_tensor("r_mr1n", [1, TOK], BF16)
    r_mbrb = nc.alloc_sbuf_tensor("r_mbrb", [1, TOK], BF16)
    r_rbf = nc.alloc_sbuf_tensor("r_rbf", [1, TOK], F32)
    t_mean = nc.alloc_sbuf_tensor("t_mean", [1, 512], F32)
    t_var = nc.alloc_sbuf_tensor("t_var", [1, 512], F32)
    t_tmp = nc.alloc_sbuf_tensor("t_tmp", [1, 512], F32)
    t_r = nc.alloc_sbuf_tensor("t_r", [1, 512], F32)
    pbig = nc.alloc_psum_tensor("pbig", [128, 2048], F32)
    pbig2 = nc.alloc_psum_tensor("pbig2", [128, 1536], F32)
    ptb = nc.alloc_psum_tensor("ptb", [128, 256], BF16)

    ctx.load_consts(["c_onescol_f", "c_onescol_b", "c_onesrow_f", "c_onesrow_b", "c_I"])
    ctx.prep_consts([1e-5, 1e-6])
    ysb2 = pool[:, 0:16384]

    ctx.dma("sq", xsb[:].rearrange("p (f t) -> p f t", f=8),
            xq.rearrange("(f p) t -> p f t", p=128), "ld")
    ctx.dma("sq", ysb2.rearrange("p (f t) -> p f t", f=8),
            yq.rearrange("(f p) t -> p f t", p=128), "ld")
    ctx.dma("sq", r_r1[0:1, :], rows1[2:3, :], "ld")
    ctx.dma("gq", r_mr1n[0:1, :], rows1[3:4, :], "gl")
    ctx.dma("sq", w8a[:], ln2w8, "ld")
    ctx.dma("sq", w8b[:], scb8, "ld")
    ctx.dma("sq", w8c[:], bias8, "ld")
    ctx.wait(nc.sync, "ld")
    ctx.wait(nc.gpsimd, "gl")
    ctx.bar()
    nc.vector.tensor_scalar_mul(w8b[:, :], w8b[:, :], 1.0 / 127.0)
    nc.vector.tensor_scalar_mul(r_mr1n[0:1, :], r_mr1n[0:1, :], -1.0)
    ctx.bar()

    def stream_w(ap, cols, kchunks, slab_off, ncols, guard=None):
        if guard is not None and ctx.cnt.get(guard, 0) > 0:
            nc.gpsimd.wait_ge(ctx.sem(guard), ctx.cnt[guard])
        ctx.dma("gq", wslab[:, slab_off:slab_off + kchunks * ncols]
                .rearrange("p (kc m) -> p kc m", kc=kchunks),
                ap[:, cols:cols + ncols].rearrange("(kc p) m -> p kc m", p=128), "wl")
        ctx.wait(nc.tensor, "wl")

    for tt in range(TOK // 512):
        ts = slice(tt * 512, (tt + 1) * 512)
        m = nc.tensor.matmul(pbig2[:, 1024:1536], ctx.c("c_onesrow_f")[0:1, :],
                             r_r1[0:1, ts], start=True, stop=True)
        for fc in range(8):
            stream_w(wo, fc * 128, 8, 0, 128, guard="mm3")
            pb = pbig[:, (fc % 2) * 512:(fc % 2) * 512 + 512]
            for kc in range(8):
                nc.tensor.matmul(pb, wslab[:, kc * 128:(kc + 1) * 128],
                                 ysb2[:, kc * TOK + tt * 512: kc * TOK + (tt + 1) * 512],
                                 start=(kc == 0), stop=False)
            mmm = nc.tensor.matmul(pb, ctx.c("c_onesrow_b")[0:1, :], r_mr1n[0:1, ts],
                                   start=False, stop=True)
            ctx.tag(mmm, "mm3")
            nc.vector.wait_ge(ctx.sem("mm3"), ctx.cnt["mm3"])
            nc.vector.tensor_tensor(scr[:, :], xsb[:, fc * TOK + tt * 512: fc * TOK + (tt + 1) * 512],
                                    pbig2[:, 1024:1536], OP.mult)
            vv = nc.vector.tensor_add(yinner[:, fc * TOK + tt * 512: fc * TOK + (tt + 1) * 512],
                                      scr[:, :], pb)
            ctx.tag(vv, "ev3")
            nc.tensor.wait_ge(ctx.sem("ev3"), ctx.cnt["ev3"])
    ctx.bar()
    dump("yinner", yinner)

    xn2 = pool[:, 0:16384]
    for tt in range(TOK // 512):
        ts = slice(tt * 512, (tt + 1) * 512)
        nc.scalar.wait_ge(ctx.sem("x23"), ctx.cnt["x23"])
        _emit_ln_sums(ctx, yinner, 8, TOK, scr, pbig2[0:1, 0:512], pbig2[0:1, 512:1024], ts)
        ctx.bar()
        nc.vector.tensor_scalar_mul(t_tmp[0:1, :], pbig2[0:1, 512:1024], 1.0 / D)
        ctx.bar()
        nc.scalar.activation(t_tmp[0:1, :], t_tmp[0:1, :], AF.Sqrt,
                             bias=ctx.cc(1e-6)[0:1, :], scale=1.0)
        ctx.bar()
        nc.vector.reciprocal(t_r[0:1, :], t_tmp[0:1, :])
        ctx.bar()
        m = nc.tensor.matmul(pbig2[:, 1024:1536], ctx.c("c_onesrow_f")[0:1, :],
                             t_r[0:1, :], start=True, stop=True)
        ctx.tag(m, "bc3")
        nc.vector.wait_ge(ctx.sem("bc3"), ctx.cnt["bc3"])
        last = None
        for fc in range(8):
            off = fc * TOK + tt * 512
            nc.vector.tensor_tensor(scr[:, :], yinner[:, off:off + 512],
                                    pbig2[:, 1024:1536], OP.mult)
            nc.vector.tensor_scalar(xn2[:, off:off + 512], scr[:, :],
                                    w8a[:, fc:fc + 1], None, OP.mult)
            last = nc.vector.tensor_add(xsb[:, off:off + 512], xsb[:, off:off + 512],
                                        yinner[:, off:off + 512])
        ctx.tag(last, "x23")
        nc.tensor.wait_ge(ctx.sem("x23"), ctx.cnt["x23"])
    ctx.bar()

    if _DBG == "xn2":
        ctx.bar()
        ctx.dma("gq", o_dbg.rearrange("(f p) t -> p f t", p=128),
                xn2.rearrange("p (f t) -> p f t", f=8), "dbg")
        ctx.wait(nc.gpsimd, "dbg")
        ctx.bar()
    for tt in range(TOK // 512):
        for hk in range(32):
            stream_w(w1, hk * 128, 8, 0, 128, guard="f1")
            pb = pbig[:, (hk % 2) * 512:(hk % 2) * 512 + 512]
            last = None
            for kc in range(8):
                last = nc.tensor.matmul(pb, wslab[:, kc * 128:(kc + 1) * 128],
                                        xn2[:, kc * TOK + tt * 512: kc * TOK + (tt + 1) * 512],
                                        start=(kc == 0), stop=(kc == 7))
            ctx.tag(last, "f1")
            nc.scalar.wait_ge(ctx.sem("f1"), ctx.cnt["f1"])
            aa = nc.scalar.activation(hbuf[:, hk * 512:(hk + 1) * 512], pb, AF.Relu,
                                      bias=0.0, scale=1.0)
            ctx.tag(aa, "f2")
            nc.tensor.wait_ge(ctx.sem("f2"), ctx.cnt["f2"])
        ctx.bar()
        for fc in range(8):
            stream_w(w2, fc * 128, 32, 0, 128, guard="f3")
            pb = pbig[:, (fc % 2) * 512:(fc % 2) * 512 + 512]
            last = None
            for kc in range(32):
                last = nc.tensor.matmul(pb, wslab[:, kc * 128:(kc + 1) * 128],
                                        hbuf[:, kc * 512: (kc + 1) * 512],
                                        start=(kc == 0), stop=(kc == 31))
            ctx.tag(last, "f3")
            nc.vector.wait_ge(ctx.sem("f3"), ctx.cnt["f3"])
            off = fc * TOK + tt * 512
            vv = nc.vector.tensor_add(xsb[:, off:off + 512], xsb[:, off:off + 512], pb)
            ctx.tag(vv, "f4")
            nc.tensor.wait_ge(ctx.sem("f4"), ctx.cnt["f4"])
        ctx.bar()

    dump("x3", xsb)
    xnb = pool[:, 0:16384]
    for tt in range(TOK // 512):
        ts = slice(tt * 512, (tt + 1) * 512)
        _emit_ln_sums(ctx, xsb, 8, TOK, scr, pbig2[0:1, 0:512], pbig2[0:1, 512:1024], ts)
        ctx.bar()
        rO = {"mean": t_mean, "var": t_var, "r1": t_r, "tmp": t_tmp, "sl": slice(0, 512)}
        _rows_chain_ln(ctx, pbig2[0:1, 0:512], pbig2[0:1, 512:1024], rO)
        nc.vector.tensor_copy(r_rbf[0:1, ts], t_r[0:1, :])
        nc.vector.tensor_mul(t_tmp[0:1, :], t_mean[0:1, :], t_r[0:1, :])
        ctx.bar()
        nc.vector.tensor_scalar_mul(r_mbrb[0:1, ts], t_tmp[0:1, :], -1.0)
        ctx.bar()
    ctx.bar()
    for fc in range(8):
        stream_w(cbt, fc * 128, 8, 0, 128)
        for kc in range(8):
            nc.tensor.matmul(pbig[0:128, fc:fc + 1],
                             wslab[:, kc * 128:(kc + 1) * 128],
                             ctx.c("c_onescol_b")[:, 0:1], start=(kc == 0), stop=(kc == 7))
        ctx.bar()
    nc.vector.tensor_copy(cwcol[:, 0:8], pbig[:, 0:8])
    ctx.bar()
    for j in range(8):
        nc.tensor.transpose(ptb[0:1, 0:128], cwcol[:, j:j + 1], ctx.c("c_I")[:])
        ctx.bar()
        nc.vector.tensor_copy(cwrow[0:1, j * 128:(j + 1) * 128], ptb[0:1, 0:128])
        ctx.bar()
    for tt in range(TOK // 512):
        ts = slice(tt * 512, (tt + 1) * 512)
        m = nc.tensor.matmul(pbig2[:, 1024:1536], ctx.c("c_onesrow_f")[0:1, :],
                             r_rbf[0:1, ts], start=True, stop=True)
        ctx.tag(m, "bcb")
        nc.vector.wait_ge(ctx.sem("bcb"), ctx.cnt["bcb"])
        last = None
        for fc in range(8):
            off = fc * TOK + tt * 512
            last = nc.vector.tensor_tensor(xnb[:, off:off + 512], xsb[:, off:off + 512],
                                           pbig2[:, 1024:1536], OP.mult)
        ctx.tag(last, "xnb")
        nc.tensor.wait_ge(ctx.sem("xnb"), ctx.cnt["xnb"])
    ctx.bar()
    for tt in range(TOK // 512):
        ts = slice(tt * 512, (tt + 1) * 512)
        for fc in range(8):
            stream_w(cbt, fc * 128, 8, 0, 128, guard="cb1")
            pb = pbig[:, (fc % 2) * 512:(fc % 2) * 512 + 512]
            for kc in range(8):
                nc.tensor.matmul(pb, wslab[:, kc * 128:(kc + 1) * 128],
                                 xnb[:, kc * TOK + tt * 512: kc * TOK + (tt + 1) * 512],
                                 start=(kc == 0), stop=False)
            mmm = nc.tensor.matmul(pb, cwrow[0:1, fc * 128:(fc + 1) * 128],
                                   r_mbrb[0:1, ts], start=False, stop=True)
            ctx.tag(mmm, "cb1")
            nc.vector.wait_ge(ctx.sem("cb1"), ctx.cnt["cb1"])
            off = fc * TOK + tt * 512
            nc.vector.tensor_scalar(scr[:, :], pb, w8b[:, fc:fc + 1], w8c[:, fc:fc + 1],
                                    OP.mult, OP.add)
            vv = nc.vector.tensor_add(xsb[:, off:off + 512], xsb[:, off:off + 512], scr[:, :])
            ctx.tag(vv, "cb2")
            nc.tensor.wait_ge(ctx.sem("cb2"), ctx.cnt["cb2"])
    ctx.bar()
    ctx.dma("sq", o_x4.rearrange("(f p) t -> p f t", p=128),
            xsb[:].rearrange("p (f t) -> p f t", f=8), "st")
    ctx.wait(nc.sync, "st")
    ctx.bar()
    return nc


def build_p4():
    nc = bass.Bass("TRN2", target_bir_lowering=False, debug=False, detect_race_conditions=False)
    ctx = Ctx(nc)
    t = lambda n, s_, d, k: nc.dram_tensor(n, s_, d, kind=k).ap()
    xq = t("x4q", [D, TOK], F32, "ExternalInput")
    n1w8 = t("n1w8", [128, 8], F32, "ExternalInput")
    o_xn4 = t("o_xn4", [D, TOK], BF16, "ExternalOutput")
    o_rows = t("o_rows4", [4, TOK], F32, "ExternalOutput")

    xsb = nc.alloc_sbuf_tensor("xsb4", [128, 8 * TOK], F32)
    scr = nc.alloc_sbuf_tensor("scr4", [128, 512], F32)
    w8 = nc.alloc_sbuf_tensor("w84", [128, 8], F32)
    xn4 = nc.alloc_sbuf_tensor("xn4", [128, 8 * TOK], BF16)
    r_mean = nc.alloc_sbuf_tensor("r_mean4", [1, TOK], F32)
    r_var = nc.alloc_sbuf_tensor("r_var4", [1, TOK], F32)
    r_r1 = nc.alloc_sbuf_tensor("r_r14", [1, TOK], F32)
    r_tmp = nc.alloc_sbuf_tensor("r_tmp4", [1, TOK], F32)
    r_s = nc.alloc_sbuf_tensor("r_s4", [1, TOK], F32)
    r_ms = nc.alloc_sbuf_tensor("r_ms4", [1, TOK], F32)
    r_mr1 = nc.alloc_sbuf_tensor("r_mr14", [1, TOK], F32)
    psSx = nc.alloc_psum_tensor("psSx4", [1, 512], F32)
    psSxx = nc.alloc_psum_tensor("psSxx4", [1, 512], F32)
    psB = nc.alloc_psum_tensor("psB4", [128, 512], F32)

    ctx.load_consts(["c_onescol_f", "c_onesrow_f"])
    ctx.prep_consts([1e-5])
    ctx.dma("sq", xsb[:].rearrange("p (f t) -> p f t", f=8),
            xq.rearrange("(f p) t -> p f t", p=128), "ld")
    ctx.dma("sq", w8[:], n1w8, "ld")
    ctx.wait(nc.sync, "ld")
    ctx.bar()
    rO = {"mean": r_mean, "var": r_var, "r1": r_r1, "tmp": r_tmp}
    for tt in range(TOK // 512):
        tsl = slice(tt * 512, (tt + 1) * 512)
        _emit_ln_sums(ctx, xsb, 8, TOK, scr, psSx, psSxx, tsl)
        ctx.bar()
        rO["sl"] = tsl
        _rows_chain_ln(ctx, psSx, psSxx, rO)
    ctx.bar()
    nc.vector.tensor_mul(r_tmp[0:1, :], r_r1[0:1, :], r_r1[0:1, :])
    ctx.bar()
    nc.vector.tensor_mul(r_tmp[0:1, :], r_tmp[0:1, :], r_var[0:1, :])
    ctx.bar()
    nc.scalar.activation(r_tmp[0:1, :], r_tmp[0:1, :], AF.Sqrt,
                         bias=ctx.cc(1e-5)[0:1, :], scale=1.0)
    ctx.bar()
    nc.vector.reciprocal(r_s[0:1, :], r_tmp[0:1, :])
    ctx.bar()
    nc.vector.tensor_mul(r_s[0:1, :], r_s[0:1, :], r_r1[0:1, :])
    ctx.bar()
    nc.vector.tensor_mul(r_ms[0:1, :], r_s[0:1, :], r_mean[0:1, :])
    nc.vector.tensor_mul(r_mr1[0:1, :], r_r1[0:1, :], r_mean[0:1, :])
    ctx.bar()
    onesrow = ctx.c("c_onesrow_f")
    for tt in range(TOK // 512):
        tsl = slice(tt * 512, (tt + 1) * 512)
        m = nc.tensor.matmul(psB[:, 0:512], onesrow[0:1, :], r_s[0:1, tsl],
                             start=True, stop=True)
        ctx.tag(m, "bc_ev")
        nc.vector.wait_ge(ctx.sem("bc_ev"), ctx.cnt["bc_ev"])
        last = None
        for fc in range(8):
            off = fc * TOK + tt * 512
            nc.vector.tensor_tensor(scr[:, 0:512], xsb[:, off:off + 512],
                                    psB[:, 0:512], OP.mult)
            last = nc.vector.tensor_scalar(xn4[:, off:off + 512], scr[:, 0:512],
                                           w8[:, fc:fc + 1], None, OP.mult)
        ctx.tag(last, "xn_ev")
        nc.tensor.wait_ge(ctx.sem("xn_ev"), ctx.cnt["xn_ev"])
        nc.scalar.wait_ge(ctx.sem("xn_ev"), ctx.cnt["xn_ev"])
    ctx.bar()
    ctx.dma("sq", o_xn4.rearrange("(f p) t -> p f t", p=128),
            xn4[:].rearrange("p (f t) -> p f t", f=8), "st")
    ctx.dma("sq", o_rows[0:1, :], r_s[0:1, :], "st")
    ctx.dma("sq", o_rows[1:2, :], r_ms[0:1, :], "st")
    ctx.dma("sq", o_rows[2:3, :], r_r1[0:1, :], "st")
    ctx.dma("sq", o_rows[3:4, :], r_mr1[0:1, :], "st")
    ctx.wait(nc.sync, "st")
    ctx.bar()
    return nc


def build_p5():
    nc = bass.Bass("TRN2", target_bir_lowering=False, debug=False, detect_race_conditions=False)
    ctx = Ctx(nc)
    t = lambda n, s_, d, k: nc.dram_tensor(n, s_, d, kind=k).ap()
    xn4f = t("xn4f", [4, D, TOK], BF16, "ExternalInput")
    mscf = t("mscf", [1, S], F32, "ExternalInput")
    win_z = t("win_z", [D, 512], F32, "ExternalInput")
    win_x = t("win_x", [D, 512], F32, "ExternalInput")
    win_B = t("win_B", [D, 512], F32, "ExternalInput")
    win_C = t("win_C", [D, 512], F32, "ExternalInput")
    win_dt = t("win_dt", [D, 8], F32, "ExternalInput")
    win_th = t("win_th", [D, 256], F32, "ExternalInput")
    n1w8 = t("n1w8", [128, 8], F32, "ExternalInput")
    alog8 = t("alog8", [8, 1], F32, "ExternalInput")
    dtb8 = t("dtb8", [8, 1], F32, "ExternalInput")
    d8 = t("d8", [1, 8], F32, "ExternalInput")
    bnw = t("bnw", [1, 64], F32, "ExternalInput")
    cnw = t("cnw", [1, 64], F32, "ExternalInput")
    bbias = t("bbias", [1, 512], F32, "ExternalInput")
    cbias = t("cbias", [1, 512], F32, "ExternalInput")
    o_yg = t("o_yg", [512, S], BF16, "ExternalOutput")
    o_dbga = t("o_dbga", [88, 512], F32, "ExternalOutput")
    o_dbgb = t("o_dbgb", [128, 2048], F32, "ExternalOutput")
    import os
    _DBGB = os.environ.get("P5_DBGB", "")

    wz = nc.alloc_sbuf_tensor("wz", [128, 8 * 512], BF16)
    wx = nc.alloc_sbuf_tensor("wx", [128, 8 * 512], BF16)
    wB = nc.alloc_sbuf_tensor("wB", [128, 8 * 512], BF16)
    wC = nc.alloc_sbuf_tensor("wC", [128, 8 * 512], BF16)
    wdt = nc.alloc_sbuf_tensor("wdt", [128, 8 * 8], BF16)
    wth = nc.alloc_sbuf_tensor("wth", [128, 8 * 256], BF16)
    n1b = nc.alloc_sbuf_tensor("n1b", [128, 8], BF16)
    n1f = nc.alloc_sbuf_tensor("n1f", [128, 8], F32)
    msneg = nc.alloc_sbuf_tensor("msneg5", [1, S], BF16)
    xt = [nc.alloc_sbuf_tensor(f"xt5{i}", [128, 8 * 512], BF16) for i in range(2)]
    btok = nc.alloc_sbuf_tensor("btok", [128, 2048], BF16)
    ctok = nc.alloc_sbuf_tensor("ctok", [128, 2048], BF16)
    cfm = nc.alloc_sbuf_tensor("cfm", [128, 2048], BF16)
    xg = nc.alloc_sbuf_tensor("xg", [128, 2048], BF16)
    xw = nc.alloc_sbuf_tensor("xw", [128, 2048], BF16)
    zsil = nc.alloc_sbuf_tensor("zsil", [128, 2048], BF16)
    ygt = nc.alloc_sbuf_tensor("ygt", [128, 2048], BF16)
    rdt = nc.alloc_sbuf_tensor("rdt", [8, 512], F32)
    rtmp = nc.alloc_sbuf_tensor("rtmp", [8, 512], F32)
    rtmp2 = nc.alloc_sbuf_tensor("rtmp2", [8, 512], F32)
    rA = nc.alloc_sbuf_tensor("rA", [8, 512], F32)
    rgam = nc.alloc_sbuf_tensor("rgam", [8, 512], F32)
    rF = nc.alloc_sbuf_tensor("rF", [8, 512], F32)
    rG = nc.alloc_sbuf_tensor("rG", [8, 512], F32)
    rea = nc.alloc_sbuf_tensor("rea", [8, 512], F32)
    ren = nc.alloc_sbuf_tensor("ren", [8, 512], F32)
    rvin = nc.alloc_sbuf_tensor("rvin", [8, 512], F32)
    ru = nc.alloc_sbuf_tensor("ru", [8, 512], F32)
    rub = nc.alloc_sbuf_tensor("rub", [8, 512], BF16)
    reab = nc.alloc_sbuf_tensor("reab", [8, 512], BF16)
    rdsv = nc.alloc_sbuf_tensor("rdsv", [8, 512], F32)
    rones = nc.alloc_sbuf_tensor("rones", [8, 512], F32)
    ramp8 = nc.alloc_sbuf_tensor("ramp8", [8, 512], F32)
    eaend = nc.alloc_sbuf_tensor("eaend", [8, 8], F32)
    prevend = nc.alloc_sbuf_tensor("prevend", [8, 8], F32)
    carryF = nc.alloc_sbuf_tensor("carryF", [8, 1], F32)
    expal = nc.alloc_sbuf_tensor("expal", [8, 1], F32)
    dtbc = nc.alloc_sbuf_tensor("dtbc", [8, 1], F32)
    dttm = nc.alloc_sbuf_tensor("dttm", [128, 32], F32)
    gamtm = nc.alloc_sbuf_tensor("gamtm", [128, 32], F32)
    dswtm = nc.alloc_sbuf_tensor("dswtm", [128, 32], F32)
    skiptok = nc.alloc_sbuf_tensor("skiptok", [128, 8], F32)
    skipfm = nc.alloc_sbuf_tensor("skipfm", [8, 512], F32)
    d8bc = nc.alloc_sbuf_tensor("d8bc", [128, 8], F32)
    bnw128 = nc.alloc_sbuf_tensor("bnw128", [128, 64], F32)
    cnw128 = nc.alloc_sbuf_tensor("cnw128", [128, 64], F32)
    bb128 = nc.alloc_sbuf_tensor("bb128", [128, 512], F32)
    cb128 = nc.alloc_sbuf_tensor("cb128", [128, 512], F32)
    dthe = nc.alloc_sbuf_tensor("dthe", [128, 256], F32)
    scrA = nc.alloc_sbuf_tensor("scrA5", [128, 512], F32)
    scrB = nc.alloc_sbuf_tensor("scrB5", [128, 512], F32)
    ssin = nc.alloc_sbuf_tensor("ssin", [128, 256], F32)
    scos = nc.alloc_sbuf_tensor("scos", [128, 256], F32)
    carryTh = nc.alloc_sbuf_tensor("carryTh", [1, 256], F32)
    rstd8 = nc.alloc_sbuf_tensor("rstd8", [128, 8], F32)
    sq5 = nc.alloc_sbuf_tensor("sq5", [128, 512], BF16)
    FS = nc.alloc_sbuf_tensor("FS", [128, 256], F32)
    FSb = nc.alloc_sbuf_tensor("FSb", [128, 256], BF16)
    T1sb = nc.alloc_sbuf_tensor("T1sb", [128, 256], BF16)
    decrow = nc.alloc_sbuf_tensor("decrow", [1, 8], F32)
    Bv = nc.alloc_sbuf_tensor("Bv", [8, 256], BF16)
    decbc = nc.alloc_sbuf_tensor("decbc", [128, 256], BF16)
    ubc = nc.alloc_sbuf_tensor("ubcs", [128, 256], BF16)
    eabc = nc.alloc_sbuf_tensor("eabcs", [128, 256], BF16)
    cwr = nc.alloc_sbuf_tensor("cwr", [1, 14 * 128], BF16)
    cwz = nc.alloc_sbuf_tensor("cwz", [1, 4 * 128], BF16)
    cwdt = nc.alloc_sbuf_tensor("cwdt", [1, 8], BF16)
    cwc = nc.alloc_sbuf_tensor("cwc", [128, 16], BF16)

    pA = nc.alloc_psum_tensor("pA", [128, 2048], F32)
    pB = nc.alloc_psum_tensor("pB", [128, 2048], F32)
    ptb = pA[:, 1792:1856].bitcast(BF16)

    ctx.load_consts(["c_onescol_f", "c_onescol_b", "c_onesrow_f", "c_onesrow_b",
                     "c_I", "c_If", "c_tril", "c_ramp", "c_mask8", "c_aexp"])
    ctx.prep_consts([1e-5])

    for w_, ap_, ncol in [(wz, win_z, 512), (wx, win_x, 512), (wB, win_B, 512),
                          (wC, win_C, 512), (wdt, win_dt, 8), (wth, win_th, 256)]:
        ctx.dma("gq", w_[:].rearrange("p (kc m) -> p kc m", kc=8),
                ap_.rearrange("(kc p) m -> p kc m", p=128), "gl")
    ctx.dma("gq", msneg[0:1, :], mscf, "gl")
    ctx.dma("sq", n1f[:], n1w8, "ld")
    ctx.dma("sq", expal[:], alog8, "ld")
    ctx.dma("sq", dtbc[:], dtb8, "ld")
    ctx.wait(nc.sync, "ld")
    ctx.wait(nc.gpsimd, "gl")
    ctx.bar()
    ctx.dma("gq", ramp8[:], ctx.cdram["c_ramp"][0:1, :].partition_broadcast(8), "g2")
    ctx.dma("gq", d8bc[:], d8[0:1, :].partition_broadcast(128), "g2")
    ctx.dma("gq", bnw128[:], bnw[0:1, :].partition_broadcast(128), "g2")
    ctx.dma("gq", cnw128[:], cnw[0:1, :].partition_broadcast(128), "g2")
    ctx.dma("gq", bb128[:], bbias[0:1, :].partition_broadcast(128), "g2")
    ctx.dma("gq", cb128[:], cbias[0:1, :].partition_broadcast(128), "g2")
    ctx.wait(nc.gpsimd, "g2")
    ctx.bar()
    nc.vector.memset(rones[:], 1.0)
    nc.vector.memset(carryF[:], 0.0)
    nc.vector.memset(carryTh[:], 0.0)
    nc.vector.memset(FS[:], 0.0)
    nc.vector.tensor_copy(n1b[:], n1f[:])
    nc.vector.tensor_scalar_mul(msneg[0:1, :], msneg[0:1, :], -1.0)
    ctx.bar()
    nc.scalar.activation(expal[:], expal[:], AF.Exp, bias=0.0, scale=1.0)
    ctx.bar()
    nc.vector.tensor_scalar_mul(expal[:], expal[:], -1.0)
    ctx.bar()

    order = [(wx, 4, 0), (wB, 4, 4), (wC, 4, 8), (wth, 2, 12)]
    for w_, nch, base in order:
        for mi in range(nch):
            for kc in range(8):
                wid = 512 if w_ is not wth else 256
                nc.tensor.matmul(pA[0:128, base + mi: base + mi + 1],
                                 w_[:, kc * wid + mi * 128: kc * wid + (mi + 1) * 128],
                                 n1b[:, kc:kc + 1], start=(kc == 0), stop=(kc == 7))
    ctx.bar()
    nc.vector.tensor_copy(cwc[:, 0:14], pA[0:128, 0:14])
    ctx.bar()
    for j in range(14):
        nc.tensor.transpose(ptb[0:1, 0:128], cwc[:, j:j + 1], ctx.c("c_I")[:])
        ctx.bar()
        nc.vector.tensor_copy(cwr[0:1, j * 128:(j + 1) * 128], ptb[0:1, 0:128])
        ctx.bar()
    for mi in range(4):
        for kc in range(8):
            nc.tensor.matmul(pA[0:128, mi:mi + 1],
                             wz[:, kc * 512 + mi * 128: kc * 512 + (mi + 1) * 128],
                             n1b[:, kc:kc + 1], start=(kc == 0), stop=(kc == 7))
    ctx.bar()
    nc.vector.tensor_copy(cwc[:, 0:4], pA[0:128, 0:4])
    ctx.bar()
    for j in range(4):
        nc.tensor.transpose(ptb[0:1, 0:128], cwc[:, j:j + 1], ctx.c("c_I")[:])
        ctx.bar()
        nc.vector.tensor_copy(cwz[0:1, j * 128:(j + 1) * 128], ptb[0:1, 0:128])
        ctx.bar()
    for kc in range(8):
        nc.tensor.matmul(pA[0:8, 0:1], wdt[:, kc * 8:(kc + 1) * 8], n1b[:, kc:kc + 1],
                         start=(kc == 0), stop=(kc == 7))
    ctx.bar()
    nc.vector.tensor_copy(cwc[0:8, 0:1], pA[0:8, 0:1])
    ctx.bar()
    nc.tensor.transpose(ptb[0:1, 0:8], cwc[0:8, 0:1], ctx.c("c_I")[0:8, 0:8])
    ctx.bar()
    nc.vector.tensor_copy(cwdt[0:1, 0:8], ptb[0:1, 0:8])
    ctx.bar()

    TPI = 2 * math.pi

    for tile in range(16):
        gts = tile * 512
        rank = gts // TOK
        toff = gts % TOK
        xb = xt[tile % 2]
        ctx.dma("sq", xb[:].rearrange("p (kc t) -> p kc t", kc=8),
                xn4f[rank, :, toff:toff + 512].rearrange("(kc p) t -> p kc t", p=128),
                "xl")
        ctx.wait(nc.sync, "xl")
        ctx.bar()
        msl = msneg[0:1, gts:gts + 512]

        for kc in range(8):
            nc.tensor.matmul(pB[0:8, 0:512], wdt[:, kc * 8:(kc + 1) * 8],
                             xb[:, kc * 512:(kc + 1) * 512], start=(kc == 0), stop=False)
        _i = nc.tensor.matmul(pB[0:8, 0:512], cwdt[0:1, 0:8], msl, start=False, stop=True)
        ctx.tag(_i, "dtmm")
        nc.vector.wait_ge(ctx.sem("dtmm"), ctx.cnt["dtmm"])
        nc.vector.tensor_scalar(rdt[:, :], pB[0:8, 0:512], dtbc[:, 0:1], None, OP.add)
        nc.scalar.wait_ge(ctx.sem("dtmm"), ctx.cnt["dtmm"])
        ctx.bar()
        nc.scalar.activation(rtmp[:, :], rdt[:, :], AF.Abs, bias=0.0, scale=1.0)
        ctx.bar()
        nc.scalar.activation(rtmp[:, :], rtmp[:, :], AF.Exp, bias=0.0, scale=-1.0)
        ctx.bar()
        nc.scalar.activation(rtmp[:, :], rtmp[:, :], AF.Ln, bias=ctx.c("c_onescol_f")[0:8, 0:1], scale=1.0)
        ctx.bar()
        nc.vector.tensor_scalar_max(rtmp2[:, :], rdt[:, :], 0.0)
        ctx.bar()
        nc.vector.tensor_add(rdt[:, :], rtmp[:, :], rtmp2[:, :])
        ctx.bar()
        nc.vector.tensor_scalar(rA[:, :], rdt[:, :], expal[:, 0:1], None, OP.mult)
        ctx.bar()
        nc.scalar.activation(rtmp[:, :], rA[:, :], AF.Exp, bias=0.0, scale=1.0)
        ctx.bar()
        nc.vector.tensor_scalar(rtmp[:, :], rtmp[:, :], -1.0, None, OP.add)
        nc.vector.tensor_scalar(rtmp2[:, :], rA[:, :], 1e-6, None, OP.add)
        ctx.bar()
        nc.vector.reciprocal(rtmp2[:, :], rtmp2[:, :])
        ctx.bar()
        nc.vector.tensor_mul(rgam[:, :], rtmp[:, :], rtmp2[:, :])
        ctx.bar()
        nc.vector.tensor_scalar(rgam[:, :], rgam[:, :], 0.5, 1.0, OP.mult, OP.add)
        ctx.bar()
        nc.vector.tensor_tensor_scan(rF[:, :], rones[:, :], rA[:, :], carryF[:, 0:1],
                                     OP.mult, OP.add)
        ctx.bar()
        nc.vector.tensor_copy(prevend[:, 0:1], carryF[:, 0:1])
        nc.vector.tensor_copy(prevend[:, 1:8],
                              rF[:, :].rearrange("p (c l) -> p c l", c=8)[:, 0:7, 63:64]
                              .rearrange("p c l -> p (c l)"))
        ctx.bar()
        nc.vector.tensor_copy(carryF[:, 0:1], rF[:, 511:512])
        ctx.bar()
        pv = prevend[:, :].unsqueeze(2).broadcast_to((8, 8, 64))
        nc.vector.tensor_tensor(rG[:, :].rearrange("p (c l) -> p c l", c=8),
                                rF[:, :].rearrange("p (c l) -> p c l", c=8), pv, OP.subtract)
        ctx.bar()
        nc.scalar.activation(rea[:, :], rG[:, :], AF.Exp, bias=0.0, scale=1.0)
        ctx.bar()
        nc.scalar.activation(ren[:, :], rG[:, :], AF.Exp, bias=0.0, scale=-1.0)
        ctx.bar()
        nc.vector.tensor_copy(eaend[:, :],
                              rea[:, :].rearrange("p (c l) -> p c l", c=8)[:, :, 63:64]
                              .rearrange("p c l -> p (c l)"))
        ctx.bar()
        for c in range(8):
            nc.vector.tensor_tensor_scan(rvin[:, c * 64:(c + 1) * 64],
                                         rones[:, 0:64], ren[:, c * 64:(c + 1) * 64],
                                         0.0, OP.mult, OP.add)
        ctx.bar()
        nc.vector.tensor_sub(rtmp[:, :], rvin[:, :], ren[:, :])
        ctx.bar()
        nc.vector.tensor_mul(rtmp[:, :], rtmp[:, :], rea[:, :])
        ctx.bar()
        nc.vector.tensor_add(ru[:, :], rtmp[:, :], ramp8[:, :])
        ctx.bar()
        nc.vector.tensor_copy(rub[:, :], ru[:, :])
        nc.vector.tensor_copy(reab[:, :], rea[:, :])
        ctx.bar()
        ev_ = eaend[:, :].unsqueeze(2).broadcast_to((8, 8, 64))
        nc.vector.tensor_tensor(rdsv[:, :].rearrange("p (c l) -> p c l", c=8),
                                ren[:, :].rearrange("p (c l) -> p c l", c=8), ev_, OP.mult)
        ctx.bar()
        if tile == 0:
            for ri, rr in enumerate([rdt, rA, rF, rG, rea, ren, rvin, ru, rgam, rdsv]):
                ctx.dma("sq", o_dbga[ri * 8:(ri + 1) * 8, :], rr[:, :], "dbg")
            ctx.dma("sq", o_dbga[80:88, :], eaend[:, 0:8].broadcast_to((8, 8)) if False else ru[:, :], "dbg")
            ctx.wait(nc.sync, "dbg")
            ctx.bar()
        for tc in range(4):
            for src, dst in [(rdt, dttm), (rgam, gamtm), (rdsv, dswtm)]:
                mt = nc.tensor.transpose(pB[0:128, 512:520].bitcast(F32)[:, 0:8] if False else pA[0:128, 1024:1032],
                                         src[:, tc * 128:(tc + 1) * 128], ctx.c("c_If")[0:8, 0:8])
                ctx.tag(mt, "ttm")
                nc.vector.wait_ge(ctx.sem("ttm"), ctx.cnt["ttm"])
                vv = nc.vector.tensor_copy(dst[:, tc * 8:(tc + 1) * 8], pA[0:128, 1024:1032])
                ctx.tag(vv, "ttc")
                nc.tensor.wait_ge(ctx.sem("ttc"), ctx.cnt["ttc"])
        ctx.bar()

        for mz in range(4):
            pb = pA[:, (mz % 2) * 512:(mz % 2) * 512 + 512]
            for kc in range(8):
                nc.tensor.matmul(pb, wz[:, kc * 512 + mz * 128: kc * 512 + (mz + 1) * 128],
                                 xb[:, kc * 512:(kc + 1) * 512], start=(kc == 0), stop=False)
            _i = nc.tensor.matmul(pb, cwz[0:1, mz * 128:(mz + 1) * 128], msl,
                                  start=False, stop=True)
            ctx.tag(_i, "zmm")
            nc.scalar.wait_ge(ctx.sem("zmm"), ctx.cnt["zmm"])
            aa = nc.scalar.activation(zsil[:, mz * 512:(mz + 1) * 512], pb, AF.Silu,
                                      bias=0.0, scale=1.0)
            ctx.tag(aa, "zsl")
            nc.tensor.wait_ge(ctx.sem("zsl"), ctx.cnt["zsl"])
        ctx.bar()

        for tc in range(4):
            tsl = slice(tc * 128, (tc + 1) * 128)
            for kc in range(8):
                st = (kc == 0)
                sp = (kc == 7)
                lhs = xb[:, kc * 512 + tc * 128: kc * 512 + (tc + 1) * 128]
                nc.tensor.matmul(pA[:, 0:512], lhs, wx[:, kc * 512:(kc + 1) * 512], start=st, stop=False)
                nc.tensor.matmul(pA[:, 512:1024], lhs, wB[:, kc * 512:(kc + 1) * 512], start=st, stop=False)
                nc.tensor.matmul(pA[:, 1024:1536], lhs, wC[:, kc * 512:(kc + 1) * 512], start=st, stop=False)
                nc.tensor.matmul(pA[:, 1536:1792], lhs, wth[:, kc * 256:(kc + 1) * 256], start=st, stop=False)
            mstat = msneg[0:1, gts + tc * 128: gts + (tc + 1) * 128]
            for regi, (off, cwbase, wd) in enumerate([(0, 0, 512), (512, 4, 512), (1024, 8, 512)]):
                nc.tensor.matmul(pA[:, off:off + wd], mstat,
                                 cwr[0:1, cwbase * 128:(cwbase + 4) * 128][0:1, 0:wd],
                                 start=False, stop=True)
            _i = nc.tensor.matmul(pA[:, 1536:1792], mstat, cwr[0:1, 12 * 128:14 * 128],
                                  start=False, stop=True)
            ctx.tag(_i, "swm")
            nc.vector.wait_ge(ctx.sem("swm"), ctx.cnt["swm"])
            if tile == 0 and _DBGB in ("braw", "craw", "xraw", "thraw"):
                srcp = {"braw": pA[:, 512:1024], "craw": pA[:, 1024:1536],
                        "xraw": pA[:, 0:512], "thraw": pA[:, 1536:1792]}[_DBGB]
                wd = 256 if _DBGB == "thraw" else 512
                nc.vector.tensor_copy(ygt[:, tc * 512: tc * 512 + wd].bitcast(F32)[:, 0:wd] if False
                                      else scrB[:, 0:wd], srcp)
                ctx.bar()
                ctx.dma("sq", o_dbgb[:, tc * 512: tc * 512 + wd], scrB[:, 0:wd], "dbg3")
                ctx.wait(nc.sync, "dbg3")
                ctx.bar()

            dtv_ = dttm[:, tc * 8:(tc + 1) * 8].unsqueeze(2).broadcast_to((128, 8, 32))
            nc.vector.tensor_tensor(dthe[:, :].rearrange("p (h j) -> p h j", h=8),
                                    pA[:, 1536:1792].rearrange("p (h j) -> p h j", h=8),
                                    dtv_, OP.mult)
            ctx.tag(nc.vector.tensor_copy(scrA[0:1, 0:1], dthe[0:1, 0:1]), "dth")
            nc.tensor.wait_ge(ctx.sem("dth"), ctx.cnt["dth"])
            m1 = nc.tensor.matmul(pB[:, 512:768], ctx.c("c_tril")[:], dthe[:, :],
                                  start=True, stop=False)
            m2 = nc.tensor.matmul(pB[:, 512:768], ctx.c("c_onesrow_f")[0:1, :],
                                  carryTh[0:1, :], start=False, stop=True)
            m3 = nc.tensor.matmul(pB[0:1, 768:1024], ctx.c("c_onescol_f")[:, 0:1],
                                  dthe[:, :], start=True, stop=True)
            ctx.tag(m3, "cum")
            nc.vector.wait_ge(ctx.sem("cum"), ctx.cnt["cum"])
            if tile == 0 and _DBGB == "cum":
                nc.vector.tensor_copy(scrB[:, 0:256], pB[:, 512:768])
                ctx.bar()
                ctx.dma("sq", o_dbgb[:, tc * 512: tc * 512 + 256], scrB[:, 0:256], "dbg3")
                ctx.wait(nc.sync, "dbg3")
                ctx.bar()
            nc.vector.tensor_add(carryTh[0:1, :], carryTh[0:1, :], pB[0:1, 768:1024])
            K_ = 1.5 * (2 ** 23)
            nc.vector.tensor_scalar(scrA[:, 0:256], pB[:, 512:768], 1.0 / TPI, K_,
                                    OP.mult, OP.add)
            ctx.bar()
            nc.vector.tensor_scalar(scrA[:, 0:256], scrA[:, 0:256], -K_, None, OP.add)
            ctx.bar()
            nc.vector.scalar_tensor_tensor(scrB[:, 0:256], scrA[:, 0:256], -TPI,
                                           pB[:, 512:768], OP.mult, OP.add)
            ctx.bar()
            nc.vector.tensor_scalar_min(scrB[:, 0:256], scrB[:, 0:256], float(math.pi))
            ctx.bar()
            nc.vector.tensor_scalar_max(scrB[:, 0:256], scrB[:, 0:256], float(-math.pi))
            ctx.bar()
            nc.scalar.activation(ssin[:, :], scrB[:, 0:256], AF.Sin, bias=0.0, scale=1.0)
            ctx.bar()
            nc.vector.tensor_scalar(scrA[:, 0:256], scrB[:, 0:256], float(math.pi / 2), None, OP.add)
            ctx.bar()
            nc.vector.tensor_scalar(scrB[:, 0:256], scrA[:, 0:256], float(math.pi), None, OP.is_gt)
            ctx.bar()
            nc.vector.scalar_tensor_tensor(scrA[:, 0:256], scrB[:, 0:256], -TPI,
                                           scrA[:, 0:256], OP.mult, OP.add)
            ctx.bar()
            nc.scalar.activation(scos[:, :], scrA[:, 0:256], AF.Sin, bias=0.0, scale=1.0)
            ctx.bar()

            for (pso, w128, bias128, dstt) in [(512, bnw128, bb128, btok),
                                               (1024, cnw128, cb128, ctok)]:
                ps_ = pA[:, pso:pso + 512]
                nc.scalar.activation(sq5[:, :], ps_, AF.Square, bias=0.0, scale=1.0)
                ctx.bar()
                nc.vector.tensor_reduce(rstd8[:, :].rearrange("p (h o) -> p h o", h=8),
                                        sq5[:, :].rearrange("p (h n) -> p h n", h=8),
                                        AX, OP.add)
                ctx.bar()
                nc.vector.tensor_scalar_mul(rstd8[:, :], rstd8[:, :], 1.0 / 64.0)
                ctx.bar()
                nc.scalar.activation(rstd8[:, :], rstd8[:, :], AF.Sqrt,
                                     bias=ctx.cc(1e-5)[0:128, 0:1], scale=1.0)
                ctx.bar()
                nc.vector.reciprocal(rstd8[:, :], rstd8[:, :])
                ctx.bar()
                rsv = rstd8[:, :].unsqueeze(2).broadcast_to((128, 8, 64))
                nc.vector.tensor_tensor(scrA[:, :].rearrange("p (h n) -> p h n", h=8),
                                        ps_.rearrange("p (h n) -> p h n", h=8), rsv, OP.mult)
                ctx.bar()
                wv_ = w128[:, :].unsqueeze(1).broadcast_to((128, 8, 64))
                nc.vector.tensor_tensor(scrA[:, :].rearrange("p (h n) -> p h n", h=8),
                                        scrA[:, :].rearrange("p (h n) -> p h n", h=8), wv_, OP.mult)
                ctx.bar()
                nc.vector.tensor_add(scrA[:, :], scrA[:, :], bias128[:, :])
                ctx.bar()
                bev = scrA[:, :].rearrange("p (h t) -> p h t", h=8)[:, :, 0:32]
                bod = scrA[:, :].rearrange("p (h t) -> p h t", h=8)[:, :, 32:64]
                cv = scos[:, :].rearrange("p (h j) -> p h j", h=8)
                sv = ssin[:, :].rearrange("p (h j) -> p h j", h=8)
                dev = dstt[:, tc * 512:(tc + 1) * 512].rearrange("p (h t) -> p h t", h=8)[:, :, 0:32]
                dod = dstt[:, tc * 512:(tc + 1) * 512].rearrange("p (h t) -> p h t", h=8)[:, :, 32:64]
                nc.vector.tensor_tensor(scrB[:, 0:256].rearrange("p (h j) -> p h j", h=8), bev, cv, OP.mult)
                nc.vector.tensor_tensor(scrB[:, 256:512].rearrange("p (h j) -> p h j", h=8), bod, sv, OP.mult)
                ctx.bar()
                nc.vector.tensor_sub(dev.rearrange("p h j -> p (h j)") if False else scrB[:, 0:256],
                                     scrB[:, 0:256], scrB[:, 256:512])
                ctx.bar()
                nc.vector.tensor_copy(dev, scrB[:, 0:256].rearrange("p (h j) -> p h j", h=8))
                ctx.bar()
                nc.vector.tensor_tensor(scrB[:, 0:256].rearrange("p (h j) -> p h j", h=8), bev, sv, OP.mult)
                nc.vector.tensor_tensor(scrB[:, 256:512].rearrange("p (h j) -> p h j", h=8), bod, cv, OP.mult)
                ctx.bar()
                nc.vector.tensor_add(scrB[:, 0:256], scrB[:, 0:256], scrB[:, 256:512])
                ctx.bar()
                nc.vector.tensor_copy(dod, scrB[:, 0:256].rearrange("p (h j) -> p h j", h=8))
                ctx.bar()

            gv_ = gamtm[:, tc * 8:(tc + 1) * 8].unsqueeze(2).broadcast_to((128, 8, 64))
            nc.vector.tensor_tensor(xg[:, tc * 512:(tc + 1) * 512].rearrange("p (h q) -> p h q", h=8),
                                    pA[:, 0:512].rearrange("p (h q) -> p h q", h=8), gv_, OP.mult)
            ctx.bar()
            dv_ = dswtm[:, tc * 8:(tc + 1) * 8].unsqueeze(2).broadcast_to((128, 8, 64))
            nc.vector.tensor_tensor(xw[:, tc * 512:(tc + 1) * 512].rearrange("p (h q) -> p h q", h=8),
                                    xg[:, tc * 512:(tc + 1) * 512].rearrange("p (h q) -> p h q", h=8),
                                    dv_, OP.mult)
            ctx.bar()
            nc.vector.tensor_reduce(skiptok[:, :].rearrange("p (h o) -> p h o", h=8),
                                    pA[:, 0:512].rearrange("p (h q) -> p h q", h=8), AX, OP.add)
            ctx.bar()
            nc.vector.tensor_mul(skiptok[:, :], skiptok[:, :], d8bc[:, :])
            ctx.bar()
            mt = nc.tensor.transpose(pA[0:8, 1792:1920], skiptok[:, :], ctx.c("c_If")[0:128, 0:128])
            ctx.tag(mt, "skt")
            nc.vector.wait_ge(ctx.sem("skt"), ctx.cnt["skt"])
            nc.vector.tensor_copy(skipfm[0:8, tc * 128:(tc + 1) * 128], pA[0:8, 1792:1920])
            ctx.bar()
            for j in range(4):
                mt = nc.tensor.transpose(ptb[:, 0:128],
                                         ctok[:, tc * 512 + j * 128: tc * 512 + (j + 1) * 128],
                                         ctx.c("c_I")[:])
                ctx.tag(mt, "cft")
                nc.vector.wait_ge(ctx.sem("cft"), ctx.cnt["cft"])
                vv = nc.vector.tensor_copy(cfm[:, (j * 4 + tc) * 128:(j * 4 + tc + 1) * 128],
                                           ptb[:, 0:128])
                ctx.tag(vv, "cfc")
                nc.tensor.wait_ge(ctx.sem("cfc"), ctx.cnt["cfc"])
        ctx.bar()

        for c in range(8):
            tcb = c // 2
            pr = (c % 2) * 64
            last = None
            for h in range(8):
                lhs = btok[pr:pr + 64, tcb * 512 + h * 64: tcb * 512 + (h + 1) * 64]
                o1 = pB[(h % 2) * 64:(h % 2) * 64 + 64, 0 + (h // 2) * 64: 0 + (h // 2) * 64 + 64]
                o2 = pB[(h % 2) * 64:(h % 2) * 64 + 64, 256 + (h // 2) * 64: 256 + (h // 2) * 64 + 64]
                nc.tensor.matmul(o1, lhs, xg[pr:pr + 64, tcb * 512 + h * 64: tcb * 512 + (h + 1) * 64],
                                 start=(h < 2), stop=True, skip_group_check=True)
                last = nc.tensor.matmul(o2, lhs, xw[pr:pr + 64, tcb * 512 + h * 64: tcb * 512 + (h + 1) * 64],
                                        start=(h < 2), stop=True, skip_group_check=True)
            ctx.tag(last, "t1s")
            nc.vector.wait_ge(ctx.sem("t1s"), ctx.cnt["t1s"])
            nc.vector.tensor_copy(T1sb[:, :], pB[:, 0:256])
            mt = nc.tensor.transpose(pA[0:1, 1792:1800], eaend[:, c:c + 1], ctx.c("c_If")[0:8, 0:8])
            ctx.tag(mt, "dcr")
            nc.vector.wait_ge(ctx.sem("dcr"), ctx.cnt["dcr"])
            nc.vector.tensor_copy(decrow[0:1, :], pA[0:1, 1792:1800])
            ctx.bar()
            nc.vector.tensor_scalar(Bv[:, :], ctx.c("c_mask8")[:, :], decrow[0:1, 0:8].transpose([1, 0]) if False else eaend[:, c:c + 1], None, OP.mult)
            ctx.bar()
            mdc = nc.tensor.matmul(pA[:, 1024:1280], ctx.c("c_aexp")[:, :], Bv[:, :],
                                   start=True, stop=True)
            ctx.tag(mdc, "dcb")
            nc.vector.wait_ge(ctx.sem("dcb"), ctx.cnt["dcb"])
            nc.vector.tensor_copy(decbc[:, :], pA[:, 1024:1280])
            ctx.bar()
            nc.vector.tensor_mul(FS[:, :], FS[:, :], decbc[:, :])
            ctx.bar()
            nc.vector.tensor_add(FS[:, :], FS[:, :], pB[:, 256:512])
            ctx.bar()
            nc.vector.tensor_copy(FSb[:, :], FS[:, :])
            ctx.bar()
            last = None
            for h in range(8):
                rhs = cfm[(h % 2) * 64:(h % 2) * 64 + 64,
                          ((h // 2) * 4 + tcb) * 128 + pr: ((h // 2) * 4 + tcb) * 128 + pr + 64]
                oy1 = pB[(h % 2) * 64:(h % 2) * 64 + 64, 512 + (h // 2) * 64: 512 + (h // 2) * 64 + 64]
                oy2 = pB[(h % 2) * 64:(h % 2) * 64 + 64, 768 + (h // 2) * 64: 768 + (h // 2) * 64 + 64]
                nc.tensor.matmul(oy1, T1sb[(h % 2) * 64:(h % 2) * 64 + 64, (h // 2) * 64:(h // 2) * 64 + 64],
                                 rhs, start=(h < 2), stop=True, skip_group_check=True)
                last = nc.tensor.matmul(oy2, FSb[(h % 2) * 64:(h % 2) * 64 + 64, (h // 2) * 64:(h // 2) * 64 + 64],
                                        rhs, start=(h < 2), stop=True, skip_group_check=True)
            ctx.tag(last, "ymm")
            uv_ = rub[:, c * 64:(c + 1) * 64].unsqueeze(1).broadcast_to((8, 4, 64))
            nc.vector.tensor_tensor(Bv[:, :].rearrange("p (j l) -> p j l", j=4), uv_,
                                    ctx.c("c_mask8")[:, :].rearrange("p (j l) -> p j l", j=4), OP.mult)
            ctx.bar()
            mu = nc.tensor.matmul(pA[:, 1024:1280], ctx.c("c_aexp")[:, :], Bv[:, :],
                                  start=True, stop=True)
            ctx.tag(mu, "ubm")
            nc.vector.wait_ge(ctx.sem("ubm"), ctx.cnt["ubm"])
            nc.vector.tensor_copy(ubc[:, :], pA[:, 1024:1280])
            ctx.bar()
            ev2 = reab[:, c * 64:(c + 1) * 64].unsqueeze(1).broadcast_to((8, 4, 64))
            nc.vector.tensor_tensor(Bv[:, :].rearrange("p (j l) -> p j l", j=4), ev2,
                                    ctx.c("c_mask8")[:, :].rearrange("p (j l) -> p j l", j=4), OP.mult)
            ctx.bar()
            me = nc.tensor.matmul(pA[:, 1280:1536], ctx.c("c_aexp")[:, :], Bv[:, :],
                                  start=True, stop=True)
            ctx.tag(me, "ebm")
            nc.vector.wait_ge(ctx.sem("ebm"), ctx.cnt["ebm"])
            nc.vector.tensor_copy(eabc[:, :], pA[:, 1280:1536])
            ctx.bar()
            sv2 = skipfm[:, tcb * 128 + pr: tcb * 128 + pr + 64].unsqueeze(1).broadcast_to((8, 4, 64))
            nc.vector.tensor_tensor(Bv[:, :].rearrange("p (j l) -> p j l", j=4), sv2,
                                    ctx.c("c_mask8")[:, :].rearrange("p (j l) -> p j l", j=4), OP.mult)
            ctx.bar()
            ms_ = nc.tensor.matmul(pA[:, 1536:1792], ctx.c("c_aexp")[:, :], Bv[:, :],
                                   start=True, stop=True)
            ctx.tag(ms_, "skm")
            nc.vector.wait_ge(ctx.sem("skm"), ctx.cnt["skm"])
            nc.vector.wait_ge(ctx.sem("ymm"), ctx.cnt["ymm"])
            nc.vector.tensor_tensor(scrA[:, 0:256], pB[:, 512:768], ubc[:, :], OP.mult)
            ctx.bar()
            nc.vector.tensor_tensor(scrB[:, 0:256], pB[:, 768:1024], eabc[:, :], OP.mult)
            ctx.bar()
            nc.vector.tensor_add(scrA[:, 0:256], scrA[:, 0:256], scrB[:, 0:256])
            ctx.bar()
            nc.vector.tensor_add(scrA[:, 0:256], scrA[:, 0:256], pA[:, 1536:1792])
            ctx.bar()
            zv = zsil[:, :].rearrange("p (j t) -> p j t", j=4)[:, :, c * 64:(c + 1) * 64]
            last = nc.vector.tensor_tensor(ygt[:, :].rearrange("p (j t) -> p j t", j=4)[:, :, c * 64:(c + 1) * 64],
                                           scrA[:, 0:256].rearrange("p (j l) -> p j l", j=4),
                                           zv, OP.mult)
            ctx.tag(last, "ygd")
            nc.tensor.wait_ge(ctx.sem("ygd"), ctx.cnt["ygd"])
        ctx.bar()
        if tile == 0 and _DBGB in ("btok", "ctok", "cfm", "xg", "xw", "zsil", "ygt"):
            srcmap = {"btok": btok, "ctok": ctok, "cfm": cfm, "xg": xg, "xw": xw,
                      "zsil": zsil, "ygt": ygt}
            ctx.dma("gq", o_dbgb, srcmap[_DBGB][:, 0:2048], "dbg2")
            ctx.wait(nc.gpsimd, "dbg2")
            ctx.bar()
        ctx.dma("sq", o_yg.rearrange("(j q) t -> q j t", j=4)[:, :, gts:gts + 512],
                ygt[:].rearrange("q (j t) -> q j t", j=4), "st")
        ctx.wait(nc.sync, "st")
        ctx.bar()
    return nc


def build_p6():
    nc = bass.Bass("TRN2", target_bir_lowering=False, debug=False, detect_race_conditions=False)
    ctx = Ctx(nc)
    t = lambda n, s_, d, k: nc.dram_tensor(n, s_, d, kind=k).ap()
    ygq = t("ygq", [2048, TOK], BF16, "ExternalInput")
    xq = t("x4q", [D, TOK], F32, "ExternalInput")
    rows4 = t("rows4", [4, TOK], F32, "ExternalInput")
    wout = t("wout", [2048, D], F32, "ExternalInput")
    wg = t("wg", [D, DMLP], F32, "ExternalInput")
    wu = t("wu", [D, DMLP], F32, "ExternalInput")
    wd = t("wd", [DMLP, D], F32, "ExternalInput")
    n2w8 = t("n2w8", [128, 8], F32, "ExternalInput")
    o_x5 = t("o_x5", [D, TOK], F32, "ExternalOutput")

    xsb = nc.alloc_sbuf_tensor("xsb6", [128, 8 * TOK], F32)
    yin2 = nc.alloc_sbuf_tensor("yin2", [128, 8 * TOK], F32)
    tbuf = yin2[:].bitcast(BF16)[:, 0:20 * 512]
    pool = nc.alloc_sbuf_tensor("pool6", [128, 16384], BF16)
    ygt = nc.alloc_sbuf_tensor("ygt6", [128, 16 * 512], BF16)
    wslab = nc.alloc_sbuf_tensor("wslab6", [128, 4096], BF16)
    scr = nc.alloc_sbuf_tensor("scr6", [128, 512], F32)
    gtile = nc.alloc_sbuf_tensor("gtile", [128, 512], BF16)
    w8a = nc.alloc_sbuf_tensor("w8a6", [128, 8], F32)
    r_r1 = nc.alloc_sbuf_tensor("r_r16", [1, TOK], F32)
    r_mr1n = nc.alloc_sbuf_tensor("r_mr1n6", [1, TOK], BF16)
    t_tmp = nc.alloc_sbuf_tensor("t_tmp6", [1, 512], F32)
    t_r = nc.alloc_sbuf_tensor("t_r6", [1, 512], F32)
    pbig = nc.alloc_psum_tensor("pbig6", [128, 2048], F32)
    pbig2 = nc.alloc_psum_tensor("pbig26", [128, 1536], F32)

    ctx.load_consts(["c_onescol_f", "c_onescol_b", "c_onesrow_f", "c_onesrow_b", "c_I"])
    ctx.prep_consts([1e-5])
    ctx.dma("sq", xsb[:].rearrange("p (f t) -> p f t", f=8),
            xq.rearrange("(f p) t -> p f t", p=128), "ld")
    ctx.dma("sq", r_r1[0:1, :], rows4[2:3, :], "ld")
    ctx.dma("gq", r_mr1n[0:1, :], rows4[3:4, :], "gl")
    ctx.dma("sq", w8a[:], n2w8, "ld")
    ctx.wait(nc.sync, "ld")
    ctx.wait(nc.gpsimd, "gl")
    ctx.bar()
    nc.vector.tensor_scalar_mul(r_mr1n[0:1, :], r_mr1n[0:1, :], -1.0)
    ctx.bar()

    def stream_w(ap, cols, kchunks, ncols, guard=None):
        if guard is not None and ctx.cnt.get(guard, 0) > 0:
            nc.gpsimd.wait_ge(ctx.sem(guard), ctx.cnt[guard])
        ctx.dma("gq", wslab[:, 0:kchunks * ncols].rearrange("p (kc m) -> p kc m", kc=kchunks),
                ap[:, cols:cols + ncols].rearrange("(kc p) m -> p kc m", p=128), "wl")
        ctx.wait(nc.tensor, "wl")

    for tt in range(TOK // 512):
        ts = slice(tt * 512, (tt + 1) * 512)
        ctx.dma("sq", ygt[:].rearrange("p (kc t) -> p kc t", kc=16),
                ygq[:, ts].rearrange("(kc p) t -> p kc t", p=128), "yl")
        ctx.wait(nc.tensor, "yl")
        m = nc.tensor.matmul(pbig2[:, 1024:1536], ctx.c("c_onesrow_f")[0:1, :],
                             r_r1[0:1, ts], start=True, stop=True)
        for fc in range(8):
            stream_w(wout, fc * 128, 16, 128, guard="mo")
            pb = pbig[:, (fc % 2) * 512:(fc % 2) * 512 + 512]
            for kc in range(16):
                nc.tensor.matmul(pb, wslab[:, kc * 128:(kc + 1) * 128],
                                 ygt[:, kc * 512:(kc + 1) * 512],
                                 start=(kc == 0), stop=False)
            mmm = nc.tensor.matmul(pb, ctx.c("c_onesrow_b")[0:1, :], r_mr1n[0:1, ts],
                                   start=False, stop=True)
            ctx.tag(mmm, "mo")
            nc.vector.wait_ge(ctx.sem("mo"), ctx.cnt["mo"])
            off = fc * TOK + tt * 512
            nc.vector.tensor_tensor(scr[:, :], xsb[:, off:off + 512], pbig2[:, 1024:1536], OP.mult)
            vv = nc.vector.tensor_add(yin2[:, off:off + 512], scr[:, :], pb)
            ctx.tag(vv, "ev6")
            nc.tensor.wait_ge(ctx.sem("ev6"), ctx.cnt["ev6"])
        ctx.bar()

    xn5 = pool[:, 0:16384]
    for tt in range(TOK // 512):
        ts = slice(tt * 512, (tt + 1) * 512)
        nc.scalar.wait_ge(ctx.sem("x26"), ctx.cnt.get("x26", 0))
        _emit_ln_sums(ctx, yin2, 8, TOK, scr, pbig2[0:1, 0:512], pbig2[0:1, 512:1024], ts)
        ctx.bar()
        nc.vector.tensor_scalar_mul(t_tmp[0:1, :], pbig2[0:1, 512:1024], 1.0 / D)
        ctx.bar()
        nc.scalar.activation(t_tmp[0:1, :], t_tmp[0:1, :], AF.Sqrt,
                             bias=ctx.cc(1e-5)[0:1, :], scale=1.0)
        ctx.bar()
        nc.vector.reciprocal(t_r[0:1, :], t_tmp[0:1, :])
        ctx.bar()
        m = nc.tensor.matmul(pbig2[:, 1024:1536], ctx.c("c_onesrow_f")[0:1, :],
                             t_r[0:1, :], start=True, stop=True)
        ctx.tag(m, "bc6")
        nc.vector.wait_ge(ctx.sem("bc6"), ctx.cnt["bc6"])
        last = None
        for fc in range(8):
            off = fc * TOK + tt * 512
            nc.vector.tensor_tensor(scr[:, :], yin2[:, off:off + 512],
                                    pbig2[:, 1024:1536], OP.mult)
            nc.vector.tensor_scalar(xn5[:, off:off + 512], scr[:, :],
                                    w8a[:, fc:fc + 1], None, OP.mult)
            last = nc.vector.tensor_add(xsb[:, off:off + 512], xsb[:, off:off + 512],
                                        yin2[:, off:off + 512])
        ctx.tag(last, "x26")
        nc.tensor.wait_ge(ctx.sem("x26"), ctx.cnt["x26"])
    ctx.bar()

    for tt in range(TOK // 512):
        for hk in range(20):
            stream_w(wg, hk * 128, 8, 128, guard="g1")
            pb = pbig[:, 0:512]
            last = None
            for kc in range(8):
                last = nc.tensor.matmul(pb, wslab[:, kc * 128:(kc + 1) * 128],
                                        xn5[:, kc * TOK + tt * 512: kc * TOK + (tt + 1) * 512],
                                        start=(kc == 0), stop=(kc == 7))
            ctx.tag(last, "g1")
            stream_w(wu, hk * 128, 8, 128, guard="g2")
            pb2 = pbig[:, 512:1024]
            last = None
            for kc in range(8):
                last = nc.tensor.matmul(pb2, wslab[:, kc * 128:(kc + 1) * 128],
                                        xn5[:, kc * TOK + tt * 512: kc * TOK + (tt + 1) * 512],
                                        start=(kc == 0), stop=(kc == 7))
            ctx.tag(last, "g2")
            nc.scalar.wait_ge(ctx.sem("g1"), ctx.cnt["g1"])
            aa = nc.scalar.activation(gtile[:, :], pb, AF.Silu, bias=0.0, scale=1.0)
            ctx.tag(aa, "g3")
            nc.vector.wait_ge(ctx.sem("g3"), ctx.cnt["g3"])
            nc.vector.wait_ge(ctx.sem("g2"), ctx.cnt["g2"])
            vv = nc.vector.tensor_tensor(tbuf[:, hk * 512:(hk + 1) * 512], pb2,
                                         gtile[:, :], OP.mult)
            ctx.tag(vv, "g4")
            nc.tensor.wait_ge(ctx.sem("g4"), ctx.cnt["g4"])
        ctx.bar()
        for fc in range(8):
            stream_w(wd, fc * 128, 20, 128, guard="g5")
            pb = pbig[:, 0:512]
            last = None
            for kc in range(20):
                last = nc.tensor.matmul(pb, wslab[:, kc * 128:(kc + 1) * 128],
                                        tbuf[:, kc * 512:(kc + 1) * 512],
                                        start=(kc == 0), stop=(kc == 19))
            ctx.tag(last, "g5")
            nc.vector.wait_ge(ctx.sem("g5"), ctx.cnt["g5"])
            off = fc * TOK + tt * 512
            vv = nc.vector.tensor_add(xsb[:, off:off + 512], xsb[:, off:off + 512], pb)
            ctx.tag(vv, "g6")
            nc.tensor.wait_ge(ctx.sem("g6"), ctx.cnt["g6"])
        ctx.bar()
    ctx.dma("sq", o_x5.rearrange("(f p) t -> p f t", p=128),
            xsb[:].rearrange("p (f t) -> p f t", f=8), "st")
    ctx.wait(nc.sync, "st")
    ctx.bar()
    return nc


def build_p7():
    nc = bass.Bass("TRN2", target_bir_lowering=False, debug=False, detect_race_conditions=False)
    ctx = Ctx(nc)
    t = lambda n, s_, d, k: nc.dram_tensor(n, s_, d, kind=k).ap()
    absp = t("absp_all", [8, 2], F32, "ExternalInput")
    wup_sl = t("wupT_sl", [D, 1024], F32, "ExternalInput")
    wdn_sl = t("wdnT_sl", [DFF, 256], F32, "ExternalInput")
    o_up = t("o_wtup", [D, 1024], BF16, "ExternalOutput")
    o_dn = t("o_wtdn", [DFF, 256], BF16, "ExternalOutput")

    ab = nc.alloc_sbuf_tensor("ab", [8, 2], F32)
    thr = nc.alloc_sbuf_tensor("thr", [1, 2], F32)
    thrc = nc.alloc_sbuf_tensor("thrc", [128, 2], F32)
    wld = nc.alloc_sbuf_tensor("wld7", [128, 1024], F32)
    aw = nc.alloc_sbuf_tensor("aw7", [128, 1024], F32)
    sg = nc.alloc_sbuf_tensor("sg7", [128, 1024], F32)
    msk = nc.alloc_sbuf_tensor("msk7", [128, 1024], F32)
    wt = nc.alloc_sbuf_tensor("wt7", [128, 1024], BF16)
    ps = nc.alloc_psum_tensor("ps7", [128, 512], F32)

    ctx.load_consts(["c_onescol_f", "c_onesrow_f"])
    ctx.dma("sq", ab[:], absp, "ld")
    ctx.wait(nc.sync, "ld")
    ctx.bar()
    nc.vector.tensor_reduce(thr[0:1, 0:2].rearrange("p (a b) -> p a b", a=2) if False
                            else thr[0:1, 0:2], ab[0:1, 0:2], AX, OP.add) if False else None
    m = nc.tensor.matmul(ps[0:1, 0:2], ctx.c("c_onescol_f")[0:8, 0:1], ab[:, :],
                         start=True, stop=True)
    ctx.tag(m, "s7")
    nc.vector.wait_ge(ctx.sem("s7"), ctx.cnt["s7"])
    nc.vector.tensor_copy(thr[0:1, 0:2], ps[0:1, 0:2])
    ctx.bar()
    nc.vector.tensor_scalar_mul(thr[0:1, 0:1], thr[0:1, 0:1], 0.5 * 0.5 / (D * DFF))
    nc.vector.tensor_scalar_mul(thr[0:1, 1:2], thr[0:1, 1:2], 0.5 * 0.5 / (D * DFF))
    ctx.bar()
    m = nc.tensor.matmul(ps[:, 0:2], ctx.c("c_onesrow_f")[0:1, :], thr[0:1, 0:2],
                         start=True, stop=True)
    ctx.tag(m, "s8")
    nc.vector.wait_ge(ctx.sem("s8"), ctx.cnt["s8"])
    nc.vector.tensor_copy(thrc[:, 0:2], ps[:, 0:2])
    ctx.bar()
    for mi, (ap, oap, kdim, cols, col) in enumerate(
            [(wup_sl, o_up, D, 1024, 0), (wdn_sl, o_dn, DFF, 256, 1)]):
        for k in range(kdim // 128):
            ctx.dma("sq", wld[:, 0:cols], ap[k * 128:(k + 1) * 128, :], "wl")
            ctx.wait(nc.scalar, "wl")
            ctx.wait(nc.vector, "wl")
            a1 = nc.scalar.activation(aw[:, 0:cols], wld[:, 0:cols], AF.Abs, bias=0.0, scale=1.0)
            a2 = nc.scalar.activation(sg[:, 0:cols], wld[:, 0:cols], AF.Sign, bias=0.0, scale=1.0)
            ctx.tag(a2, "t7")
            nc.vector.wait_ge(ctx.sem("t7"), ctx.cnt["t7"])
            nc.vector.tensor_scalar(msk[:, 0:cols], aw[:, 0:cols], thrc[:, col:col + 1],
                                    None, OP.is_gt)
            vv = nc.vector.tensor_tensor(wt[:, 0:cols], msk[:, 0:cols], sg[:, 0:cols], OP.mult)
            ctx.tag(vv, "t8")
            nc.sync.wait_ge(ctx.sem("t8"), ctx.cnt["t8"])
            nc.scalar.wait_ge(ctx.sem("t8"), ctx.cnt["t8"])
            ctx.dma("sq", oap[k * 128:(k + 1) * 128, :], wt[:, 0:cols], "st")
    ctx.wait(nc.sync, "st")
    ctx.bar()
    return nc


def build_p8():
    nc = bass.Bass("TRN2", target_bir_lowering=False, debug=False, detect_race_conditions=False)
    ctx = Ctx(nc)
    t = lambda n, s_, d, k: nc.dram_tensor(n, s_, d, kind=k).ap()
    xq = t("x5q", [D, TOK], F32, "ExternalInput")
    wtup = t("wtupT", [D, DFF], BF16, "ExternalInput")
    wtdn = t("wtdnT", [DFF, D], BF16, "ExternalInput")
    o_out = t("o_out", [D, TOK], F32, "ExternalOutput")

    xsb = nc.alloc_sbuf_tensor("xsb8", [128, 8 * TOK], F32)
    hbuf = nc.alloc_sbuf_tensor("hbuf8", [128, 32 * 512], BF16)
    pool = nc.alloc_sbuf_tensor("pool8", [128, 16384], BF16)
    wslab = nc.alloc_sbuf_tensor("wslab8", [128, 4096], BF16)
    scr = nc.alloc_sbuf_tensor("scr8", [128, 512], F32)
    cwrow = nc.alloc_sbuf_tensor("cwrow8", [1, 32 * 128], BF16)
    cwcol = nc.alloc_sbuf_tensor("cwcol8", [128, 32], BF16)
    r_sd = nc.alloc_sbuf_tensor("r_sd", [1, TOK], F32)
    r_msdn = nc.alloc_sbuf_tensor("r_msdn", [1, TOK], BF16)
    t_mean = nc.alloc_sbuf_tensor("t_mean8", [1, 512], F32)
    t_var = nc.alloc_sbuf_tensor("t_var8", [1, 512], F32)
    t_tmp = nc.alloc_sbuf_tensor("t_tmp8", [1, 512], F32)
    t_r = nc.alloc_sbuf_tensor("t_r8", [1, 512], F32)
    pbig = nc.alloc_psum_tensor("pbig8", [128, 2048], F32)
    pbig2 = nc.alloc_psum_tensor("pbig28", [128, 1536], F32)
    ptb = nc.alloc_psum_tensor("ptb8", [128, 256], BF16)

    ctx.load_consts(["c_onescol_f", "c_onescol_b", "c_onesrow_f", "c_onesrow_b", "c_I"])
    ctx.prep_consts([1e-5, 1e-8])
    ctx.dma("sq", xsb[:].rearrange("p (f t) -> p f t", f=8),
            xq.rearrange("(f p) t -> p f t", p=128), "ld")
    ctx.wait(nc.sync, "ld")
    ctx.bar()

    def stream_w(ap, cols, kchunks, ncols, guard=None):
        if guard is not None and ctx.cnt.get(guard, 0) > 0:
            nc.gpsimd.wait_ge(ctx.sem(guard), ctx.cnt[guard])
        ctx.dma("gq", wslab[:, 0:kchunks * ncols].rearrange("p (kc m) -> p kc m", kc=kchunks),
                ap[:, cols:cols + ncols].rearrange("(kc p) m -> p kc m", p=128), "wl")
        ctx.wait(nc.tensor, "wl")

    xnd = pool[:, 0:16384]
    for tt in range(TOK // 512):
        ts = slice(tt * 512, (tt + 1) * 512)
        nc.scalar.wait_ge(ctx.sem("xn8"), ctx.cnt.get("xn8", 0))
        _emit_ln_sums(ctx, xsb, 8, TOK, scr, pbig2[0:1, 0:512], pbig2[0:1, 512:1024], ts)
        ctx.bar()
        rO = {"mean": t_mean, "var": t_var, "r1": t_r, "tmp": t_tmp, "sl": slice(0, 512)}
        _rows_chain_ln(ctx, pbig2[0:1, 0:512], pbig2[0:1, 512:1024], rO)
        nc.vector.tensor_mul(t_tmp[0:1, :], t_r[0:1, :], t_r[0:1, :])
        ctx.bar()
        nc.vector.tensor_mul(t_tmp[0:1, :], t_tmp[0:1, :], t_var[0:1, :])
        ctx.bar()
        nc.scalar.activation(t_tmp[0:1, :], t_tmp[0:1, :], AF.Sqrt, bias=0.0, scale=1.0)
        ctx.bar()
        nc.vector.tensor_scalar(t_tmp[0:1, :], t_tmp[0:1, :], 1e-8, None, OP.add)
        ctx.bar()
        nc.vector.reciprocal(t_tmp[0:1, :], t_tmp[0:1, :])
        ctx.bar()
        nc.vector.tensor_mul(t_tmp[0:1, :], t_tmp[0:1, :], t_r[0:1, :])
        ctx.bar()
        nc.vector.tensor_copy(r_sd[0:1, ts], t_tmp[0:1, :])
        nc.vector.tensor_mul(t_tmp[0:1, :], t_tmp[0:1, :], t_mean[0:1, :])
        ctx.bar()
        nc.vector.tensor_scalar_mul(r_msdn[0:1, ts], t_tmp[0:1, :], -1.0)
        ctx.bar()
        m = nc.tensor.matmul(pbig2[:, 1024:1536], ctx.c("c_onesrow_f")[0:1, :],
                             r_sd[0:1, ts], start=True, stop=True)
        ctx.tag(m, "bc8")
        nc.vector.wait_ge(ctx.sem("bc8"), ctx.cnt["bc8"])
        last = None
        for fc in range(8):
            off = fc * TOK + tt * 512
            last = nc.vector.tensor_tensor(xnd[:, off:off + 512], xsb[:, off:off + 512],
                                           pbig2[:, 1024:1536], OP.mult)
        ctx.tag(last, "xn8")
        nc.tensor.wait_ge(ctx.sem("xn8"), ctx.cnt["xn8"])
    ctx.bar()
    for fc in range(32):
        stream_w(wtup, fc * 128, 8, 128, guard="cs8")
        last = None
        for kc in range(8):
            last = nc.tensor.matmul(pbig[0:128, fc % 16: fc % 16 + 1],
                                    wslab[:, kc * 128:(kc + 1) * 128],
                                    ctx.c("c_onescol_b")[:, 0:1],
                                    start=(kc == 0 and fc % 16 == 0), stop=(kc == 7),
                                    skip_group_check=True)
        ctx.tag(last, "cs8")
        if fc % 16 == 15:
            nc.vector.wait_ge(ctx.sem("cs8"), ctx.cnt["cs8"])
            nc.vector.tensor_copy(cwcol[:, (fc // 16) * 16:(fc // 16) * 16 + 16],
                                  pbig[:, 0:16])
            ctx.tag(nc.vector.tensor_copy(scr[0:1, 0:1], cwcol[0:1, 0:1]), "cv8")
            nc.tensor.wait_ge(ctx.sem("cv8"), ctx.cnt["cv8"])
    ctx.bar()
    for j in range(32):
        nc.tensor.transpose(ptb[0:1, 0:128], cwcol[:, j:j + 1], ctx.c("c_I")[:])
        ctx.bar()
        nc.vector.tensor_copy(cwrow[0:1, j * 128:(j + 1) * 128], ptb[0:1, 0:128])
        ctx.bar()

    for tt in range(TOK // 512):
        ts = slice(tt * 512, (tt + 1) * 512)
        for hk in range(32):
            stream_w(wtup, hk * 128, 8, 128, guard="u1")
            pb = pbig[:, (hk % 2) * 512:(hk % 2) * 512 + 512]
            for kc in range(8):
                nc.tensor.matmul(pb, wslab[:, kc * 128:(kc + 1) * 128],
                                 xnd[:, kc * TOK + tt * 512: kc * TOK + (tt + 1) * 512],
                                 start=(kc == 0), stop=False)
            last = nc.tensor.matmul(pb, cwrow[0:1, hk * 128:(hk + 1) * 128],
                                    r_msdn[0:1, ts], start=False, stop=True)
            ctx.tag(last, "u1")
            nc.scalar.wait_ge(ctx.sem("u1"), ctx.cnt["u1"])
            aa = nc.scalar.activation(hbuf[:, hk * 512:(hk + 1) * 512], pb, AF.Gelu,
                                      bias=0.0, scale=1.0)
            ctx.tag(aa, "u2")
            nc.tensor.wait_ge(ctx.sem("u2"), ctx.cnt["u2"])
        ctx.bar()
        for i32 in range(1):
            pass
        for hk in range(32):
            aa = nc.scalar.activation(scr[:, :], hbuf[:, hk * 512:(hk + 1) * 512],
                                      AF.Square, bias=0.0, scale=1.0)
            ctx.tag(aa, "h1")
            nc.tensor.wait_ge(ctx.sem("h1"), ctx.cnt["h1"])
            last = nc.tensor.matmul(pbig2[0:1, 0:512], ctx.c("c_onescol_f")[:, 0:1],
                                    scr[:, :], start=(hk == 0), stop=(hk == 31))
            ctx.tag(last, "h2")
            nc.scalar.wait_ge(ctx.sem("h2"), ctx.cnt["h2"])
        ctx.bar()
        nc.vector.tensor_scalar_mul(t_tmp[0:1, :], pbig2[0:1, 0:512], 1.0 / DFF)
        ctx.bar()
        nc.scalar.activation(t_tmp[0:1, :], t_tmp[0:1, :], AF.Sqrt, bias=0.0, scale=1.0)
        ctx.bar()
        nc.vector.tensor_scalar(t_tmp[0:1, :], t_tmp[0:1, :], 1e-8, None, OP.add)
        ctx.bar()
        nc.vector.reciprocal(t_r[0:1, :], t_tmp[0:1, :])
        ctx.bar()
        m = nc.tensor.matmul(pbig2[:, 1024:1536], ctx.c("c_onesrow_f")[0:1, :],
                             t_r[0:1, :], start=True, stop=True)
        ctx.tag(m, "h3")
        nc.vector.wait_ge(ctx.sem("h3"), ctx.cnt["h3"])
        last = None
        for hk in range(32):
            last = nc.vector.tensor_tensor(hbuf[:, hk * 512:(hk + 1) * 512],
                                           hbuf[:, hk * 512:(hk + 1) * 512],
                                           pbig2[:, 1024:1536], OP.mult)
        ctx.tag(last, "h4")
        nc.tensor.wait_ge(ctx.sem("h4"), ctx.cnt["h4"])
        ctx.bar()
        for fc in range(8):
            stream_w(wtdn, fc * 128, 32, 128, guard="d1")
            pb = pbig[:, (fc % 2) * 512:(fc % 2) * 512 + 512]
            last = None
            for kc in range(32):
                last = nc.tensor.matmul(pb, wslab[:, kc * 128:(kc + 1) * 128],
                                        hbuf[:, kc * 512:(kc + 1) * 512],
                                        start=(kc == 0), stop=(kc == 31))
            ctx.tag(last, "d1")
            nc.vector.wait_ge(ctx.sem("d1"), ctx.cnt["d1"])
            off = fc * TOK + tt * 512
            vv = nc.vector.tensor_add(xsb[:, off:off + 512], xsb[:, off:off + 512], pb)
            ctx.tag(vv, "d2")
            nc.tensor.wait_ge(ctx.sem("d2"), ctx.cnt["d2"])
        ctx.bar()
    ctx.dma("sq", o_out.rearrange("(f p) t -> p f t", p=128),
            xsb[:].rearrange("p (f t) -> p f t", f=8), "st")
    ctx.wait(nc.sync, "st")
    ctx.bar()
    return nc


_CONSTK = ["c_sel", "c_maskrow", "c_J", "c_I", "c_If", "c_onescol_f",
           "c_onescol_b", "c_onesrow_f", "c_onesrow_b", "c_sel64", "c_exp2",
           "c_exppair", "c_exp8A", "c_exp8B", "c_tril", "c_ramp", "c_mask8",
           "c_aexp"]


def kernel(**inputs):
    maps = _stage_inputs(inputs)

    def sub(builder, keys, outs):
        nc = builder()
        ins = [{k: m[k] for k in keys} for m in maps]
        res = run_bass_kernel_spmd(nc, ins, core_ids=list(range(NCORES)))
        return [{o: np.asarray(r[o]) for o in outs} for r in res.results]

    AS = np.ascontiguousarray

    r1 = sub(build_p1, ["xq", "ln1w8", "wupT_sl", "wdnT_sl",
                        "c_onescol_f", "c_onesrow_f"],
             ["o_xn1", "o_rows", "o_absp"])
    for c in range(NCORES):
        g = (c // 4) * 4
        maps[c]["xn1f"] = np.stack([r1[g + i]["o_xn1"] for i in range(4)])
        maps[c]["msf"] = AS(np.concatenate([r1[g + i]["o_rows"][1] for i in range(4)])[None, :])
        maps[c]["rows1"] = r1[c]["o_rows"]
        maps[c]["absp_all"] = np.stack([r1[i]["o_absp"][0] for i in range(NCORES)])

    r2 = sub(build_p2, ["xn1f", "msf", "wq4", "wk4", "wv4", "ln1w8", "scale4",
                        "scale22", "scale4r", "relb4", "c_sel", "c_maskrow",
                        "c_J", "c_I", "c_If", "c_onescol_f", "c_onescol_b",
                        "c_onesrow_f", "c_onesrow_b", "c_sel64", "c_exp2"],
             ["o_y"])
    for c in range(NCORES):
        g = (c // 4) * 4
        q = c % 4
        yfull = np.concatenate([r2[g + i]["o_y"] for i in range(4)], axis=0)
        maps[c]["yq"] = AS(yfull[:, TOK * q: TOK * (q + 1)])

    r3 = sub(build_p3, ["yq", "xq", "rows1", "wo", "w1", "w2", "cbt", "ln2w8",
                        "scb8", "bias8", "c_onescol_f", "c_onescol_b",
                        "c_onesrow_f", "c_onesrow_b", "c_I"],
             ["o_x4"])
    for c in range(NCORES):
        maps[c]["x4q"] = r3[c]["o_x4"]

    r4 = sub(build_p4, ["x4q", "n1w8", "c_onescol_f", "c_onesrow_f"],
             ["o_xn4", "o_rows4"])
    for c in range(NCORES):
        g = (c // 4) * 4
        maps[c]["xn4f"] = np.stack([r4[g + i]["o_xn4"] for i in range(4)])
        maps[c]["mscf"] = AS(np.concatenate([r4[g + i]["o_rows4"][1] for i in range(4)])[None, :])
        maps[c]["rows4"] = r4[c]["o_rows4"]

    r5 = sub(build_p5, ["xn4f", "mscf", "win_z", "win_x", "win_B", "win_C",
                        "win_dt", "win_th", "n1w8", "alog8", "dtb8", "d8",
                        "bnw", "cnw", "bbias", "cbias", "c_onescol_f",
                        "c_onescol_b", "c_onesrow_f", "c_onesrow_b", "c_I",
                        "c_If", "c_tril", "c_ramp", "c_mask8", "c_aexp"],
             ["o_yg"])
    for c in range(NCORES):
        g = (c // 4) * 4
        q = c % 4
        ygfull = np.concatenate([r5[g + i]["o_yg"] for i in range(4)], axis=0)
        maps[c]["ygq"] = AS(ygfull[:, TOK * q: TOK * (q + 1)])

    r6 = sub(build_p6, ["ygq", "x4q", "rows4", "wout", "wg", "wu", "wd", "n2w8",
                        "c_onescol_f", "c_onescol_b", "c_onesrow_f",
                        "c_onesrow_b", "c_I"],
             ["o_x5"])
    for c in range(NCORES):
        maps[c]["x5q"] = r6[c]["o_x5"]

    r7 = sub(build_p7, ["absp_all", "wupT_sl", "wdnT_sl", "c_onescol_f",
                        "c_onesrow_f"],
             ["o_wtup", "o_wtdn"])
    wtupT = AS(np.concatenate([r7[i]["o_wtup"] for i in range(4)], axis=1))
    wtdnT = AS(np.concatenate([r7[i]["o_wtdn"] for i in range(4)], axis=1))
    for c in range(NCORES):
        maps[c]["wtupT"] = wtupT
        maps[c]["wtdnT"] = wtdnT

    r8 = sub(build_p8, ["x5q", "wtupT", "wtdnT", "c_onescol_f", "c_onescol_b",
                        "c_onesrow_f", "c_onesrow_b", "c_I"],
             ["o_out"])
    out = np.empty((B, S, D), np.float32)
    for c in range(NCORES):
        b, q = c // 4, c % 4
        out[b, TOK * q: TOK * (q + 1), :] = r8[c]["o_out"].T
    return out
